# version 15
# baseline (speedup 1.0000x reference)
import numpy as np
import ml_dtypes
from contextlib import ExitStack

import concourse.bass as bass
import concourse.mybir as mybir
from concourse.bass_utils import run_bass_kernel_spmd

F32 = mybir.dt.float32
BF16 = mybir.dt.bfloat16
AF = mybir.ActivationFunctionType
ALU = mybir.AluOpType
AX = mybir.AxisListType

NCORE = 8
TOK = 2048
NTC = 16
D = 1024
NDC = 8
SCALE = 0.125
NES = 128
NG = 64
NR = 43
RB = 2
EPS = 1e-5

STAGE = "full"
TAP = None
LEVEL = 10

AR = 103344
P0, P1, P2, P3, P4, P5 = 0, 16384, 32768, 49152, 65536, 81920
E_bias4, E_bc, E_b1p, E_tie = 98304, 98432, 98448, 98544
E_ss, E_sq, E_m, E_var, E_rstd = 98608, 98640, 98672, 98704, 98736
E_z, E_gmax, E_I128, E_ones128, E_ones1, E_zcol = 98768, 98784, 98912, 99040, 99056, 99042
E_BD, E_aloc, E_innerT, E_b2 = 99184, 100208, 102256, 102320
W_SLOT = [0, 8192]
F_W1 = [0, 1536]
F_W2 = [3072, 5120]
F_ROWB = 13312
F_WC = 13312
F_R3 = [7168, 8192, 9216, 10240]
F_PERM, F_PERMT, F_PBIG = 0, 2048, 4096
F_R2 = [0, 4096]
F_OUT = [0, 2048]

BLOCKS = []
_r0 = 0
while _r0 < NR:
    BLOCKS.append((_r0, min(RB, NR - _r0)))
    _r0 += RB
NB = len(BLOCKS)


def build_nc():
    nc = bass.Bass("TRN2", target_bir_lowering=False, debug=False,
                   detect_race_conditions=False)
    x_h = nc.dram_tensor("x", [TOK, D], F32, kind="ExternalInput")
    xres_h = nc.dram_tensor("xres", [TOK, D], F32, kind="ExternalInput")
    w_h = {}
    for nm in ["wq1t", "wk1t", "wq2t", "wk2t", "wvt", "wgt", "wpt"]:
        w_h[nm] = nc.dram_tensor(nm, [D, D], F32, kind="ExternalInput")
    bias4_h = nc.dram_tensor("bias4", [128, 32], F32, kind="ExternalInput")
    rowb3_h = nc.dram_tensor("rowb3", [1, 3 * D], F32, kind="ExternalInput")
    wc_h = nc.dram_tensor("wc", [D, NES], F32, kind="ExternalInput")
    bc_h = nc.dram_tensor("bcv", [128, 1], F32, kind="ExternalInput")
    b1p_h = nc.dram_tensor("b1p", [128, NR], F32, kind="ExternalInput")
    w1p_h = nc.dram_tensor("w1p", [128, NR, 3, 8, 32], F32, kind="ExternalInput")
    w2p_h = nc.dram_tensor("w2p", [128, NR, D], F32, kind="ExternalInput")
    b2_h = nc.dram_tensor("b2", [128, D], F32, kind="ExternalInput")
    out_h = nc.dram_tensor("out", [TOK, D], F32, kind="ExternalOutput")
    tap_h = nc.dram_tensor("tap", [128, 16384], F32, kind="ExternalOutput") if TAP else None
    tap2_h = nc.dram_tensor("tap2", [128, 16384], BF16, kind="ExternalOutput") if TAP else None
    a_dram = nc.dram_tensor("a_dram", [128, 1024], F32, kind="Internal")
    a_red = nc.dram_tensor("a_red", [128, 1024], F32, kind="Internal")
    r2d = nc.dram_tensor("r2d", [NES, NG, D], BF16, kind="Internal")
    o2s = nc.dram_tensor("o2s", [16, 8, 2, 64, 128], BF16, kind="Internal")
    w1b = nc.dram_tensor("w1b", [128, NR, 3, 8, 32], BF16, kind="Internal")
    w2b = nc.dram_tensor("w2b", [128, NR, D], BF16, kind="Internal")
    x = x_h.ap()
    out = out_h.ap()
    xo = out.rearrange("(c p) d -> p c d", p=128)
    xr = x.rearrange("(c p) d -> p c d", p=128)

    with ExitStack() as st:
        E = st.enter_context
        A = E(nc.sbuf_tensor("arena", [128, AR], BF16))
        rowb3_t = E(nc.sbuf_tensor("rowb3s", [1, 3 * D], BF16))
        pA = E(nc.psum_tensor("pA", [128, 2048], F32))
        pB = E(nc.psum_tensor("pB", [128, 2048], F32))
        sT = E(nc.semaphore("sT"))
        sV = E(nc.semaphore("sV"))
        sS = E(nc.semaphore("sS"))
        sP = E(nc.semaphore("sP"))
        sDW = E(nc.semaphore("sDW"))
        sDO = E(nc.semaphore("sDO"))
        sCC = E(nc.semaphore("sCC"))
        sR3 = [E(nc.semaphore(f"sR3{i}")) for i in range(3)]
        sR2 = [E(nc.semaphore(f"sR2{i}")) for i in range(2)]
        sOD = [E(nc.semaphore(f"sOD{i}")) for i in range(2)]
        sWS = [E(nc.semaphore(f"sWS{i}")) for i in range(2)]
        sO2S = E(nc.semaphore("sO2S"))
        sSF = E(nc.semaphore("sSF"))
        sAR = E(nc.semaphore("sAR"))
        xsem = {"R30": 0, "R31": 0, "R32": 0, "R20": 0, "R21": 0,
                "OD0": 0, "OD1": 0, "WS0": 0, "WS1": 0}
        blk = E(nc.Block())

        ct = {"T": 0, "V": 0, "S": 0, "P": 0, "DW": 0, "DO": 0}

        def res(key, n=1):
            v = [ct[key] + i + 1 for i in range(n)]
            ct[key] += n
            return v if n > 1 else v[0]

        def res16(key):
            ct[key] += 16
            return ct[key]

        def bf(off, n):
            return A[:, off:off + n]

        def f32v(off, nf):
            return A[:, off:off + 2 * nf].bitcast(F32)

        def v3(off, a, b_):
            return A[:, off:off + a * b_].rearrange("p (a b) -> p a b", b=b_)

        xt = v3(P0, NTC, D)
        xnf = v3(P0, NDC, TOK)
        Opp = v3(P0, NDC, TOK)
        ynf = v3(P0, NDC, TOK)
        xn_tok = v3(P1, NTC, D)
        q_tok = v3(P1, NTC, D)
        x2 = v3(P1, NTC, D)
        q1h = v3(P2, NDC, TOK)
        vfe = v3(P2, NDC, TOK)
        k1h = v3(P3, NDC, TOK)
        gfe = v3(P3, NDC, TOK)
        k_tok = v3(P4, NTC, D)
        yn_tok = v3(P4, NTC, D)
        wslot = [v3(P5 + W_SLOT[i], NDC, D) for i in range(2)]
        scr0 = bf(P2, 1024)
        icol = bf(P2 + 1024, 128)
        irow = bf(P2 + 1152, 128)
        I128 = bf(E_I128, 128)
        ones128 = bf(E_ones128, 1)
        ones1 = A[0:1, E_ones1:E_ones1 + 128]
        zcol = bf(E_zcol, 1)
        onef = A[:, 99044:99048].bitcast(F32)[:, 0:1]
        bias4 = f32v(E_bias4, 32)
        bcv = f32v(E_bc, 1)
        b1p = f32v(E_b1p, NR)
        tie = f32v(E_tie, 32)
        ssum = f32v(E_ss, 16)
        ssq = f32v(E_sq, 16)
        smean = f32v(E_m, 16)
        svar = f32v(E_var, 16)
        srstd = f32v(E_rstd, 16)
        zsb = f32v(E_z, 8)
        gmax = f32v(E_gmax, 64)
        BD = v3(E_BD, 8, 128)
        aloc = f32v(E_aloc, 1024)
        innerT = v3(E_innerT, 2, 32)
        b2sb = bf(E_b2, D)
        rowb3 = rowb3_t[0:1, :]
        wcsb = v3(P5 + F_WC, NDC, NES)
        perm = bf(P0 + F_PERM, TOK)
        permT = v3(P0 + F_PERMT, NTC, 128)
        pbig = v3(P0 + F_PBIG, 8, 512)
        r2blk = [v3(P2 + 4096 * i, 4, D) for i in range(4)]
        outst = [A[:, P3 + 2048 * i:P3 + 2048 * (i + 1)].bitcast(F32) for i in range(4)]
        r3 = [bf(P5 + F_R3[i], 1024) for i in range(4)]
        w1blk = [A[:, P5 + F_W1[i]:P5 + F_W1[i] + RB * 768].rearrange(
            "p (r q k f) -> p r q k f", q=3, k=8, f=32) for i in range(2)]
        w2blk = [v3(P5 + F_W2[i], RB, D) for i in range(2)]
        bdin = [A[:, P5 + 11264 + 96 * i:P5 + 11264 + 96 * (i + 1)] for i in range(2)]

        def x_ap(h, es, k):
            base = (P2 if k < 4 else P3) + (k % 4) * 4096
            return A[:, base:base + 4096].rearrange(
                "p (c j e) -> p c j e", j=4, e=128)[:, :, :, es]

        def tap(name, sem, val, ap, eng="gpsimd"):
            if TAP != name:
                return
            n = ap.shape[-1] if ap.ndim == 2 else int(np.prod(ap.shape[1:]))
            npart = ap.shape[0]
            if eng == "sync":
                @blk.sync
                def _(sy):
                    sy.wait_ge(sem, val)
                    sy.dma_start(out=tap2_h.ap()[0:npart, 0:n],
                                 in_=ap).then_inc(sDO, 16)
                    res16("DO")
            else:
                @blk.gpsimd
                def _(g):
                    g.wait_ge(sem, val)
                    g.dma_start(out=tap_h.ap()[0:npart, 0:n],
                                in_=ap).then_inc(sDW, 16)
                    res16("DW")

        dw_x = [0]
        dw_wq1 = [0]
        dw_pre = [0]
        p_const = res("P")

        dw_xchunks = []

        @blk.gpsimd
        def _(g):
            for i in range(4):
                g.dma_start(out=xt[:, 4 * i:4 * (i + 1), :],
                            in_=xr[:, 4 * i:4 * (i + 1), :]).then_inc(sDW, 16)
                dw_xchunks.append(res16("DW"))
            dw_x[0] = ct["DW"]
            g.dma_start(out=wslot[0][:, :, :],
                        in_=w_h["wq1t"].ap().rearrange("(ic p) o -> p ic o", p=128)
                        ).then_inc(sDW, 16)
            dw_wq1[0] = res16("DW")
            g.dma_start(out=b2sb, in_=b2_h.ap()).then_inc(sDW, 16)
            res16("DW")
            g.dma_start(out=rowb3, in_=rowb3_h.ap()).then_inc(sDW, 16)
            dw_pre[0] = res16("DW")
            g.iota(icol, pattern=[[1, 128]], base=0, channel_multiplier=0,
                   allow_small_or_imprecise_dtypes=True)
            g.iota(irow, pattern=[[0, 128]], base=0, channel_multiplier=1,
                   allow_small_or_imprecise_dtypes=True)
            g.iota(tie, pattern=[[1, 32]], base=0, channel_multiplier=0,
                   allow_small_or_imprecise_dtypes=True).then_inc(sP, 1)

        do_smalls = ct["DO"] + 48

        @blk.sync
        def _(sy):
            sy.dma_start(out=bias4, in_=bias4_h.ap()).then_inc(sDO, 16)
            sy.dma_start(out=bcv, in_=bc_h.ap()).then_inc(sDO, 16)
            sy.dma_start(out=b1p, in_=b1p_h.ap()).then_inc(sDO, 16)
            res16("DO"); res16("DO"); res16("DO")

        v_const = res("V")

        @blk.vector
        def _(v):
            v.wait_ge(sP, p_const)
            v.tensor_tensor(I128, icol, irow, op=ALU.is_equal)
            v.memset(ones128, 1.0)
            v.memset(ones1, 1.0)
            v.memset(zcol, 0.0)
            v.memset(onef, 1.0)
            v.tensor_scalar_mul(tie, tie, 1e-6 / 31.0).then_inc(sV, 1)

        def layer_norm(src, dst, waits, per_chunk=None):
            s_stats = res("S")

            @blk.scalar
            def _(s):
                for sem, val in waits:
                    s.wait_ge(sem, val)
                ins = None
                for c in range(NTC):
                    if per_chunk is not None:
                        s.wait_ge(per_chunk[0], per_chunk[1][c])
                    s.activation(scr0, src[:, c, :], AF.Square,
                                 accum_out=ssq[:, c:c + 1])
                    ins = s.activation(scr0, src[:, c, :], AF.Identity,
                                       accum_out=ssum[:, c:c + 1])
                ins.then_inc(sS, 1)

            vv = res("V", 4)
            v_var = vv[3]

            @blk.vector
            def _(v):
                v.wait_ge(sS, s_stats)
                v.tensor_scalar_mul(smean, ssum, 1.0 / D).then_inc(sV, 1)
                v.wait_ge(sV, vv[0])
                v.tensor_tensor(svar, smean, smean, op=ALU.mult).then_inc(sV, 1)
                v.wait_ge(sV, vv[1])
                v.scalar_tensor_tensor(svar, ssq, 1.0 / D, svar,
                                       op0=ALU.mult, op1=ALU.subtract).then_inc(sV, 1)
                v.wait_ge(sV, vv[2])
                v.tensor_scalar_add(svar, svar, EPS).then_inc(sV, 1)

            s_ln = res("S")

            @blk.scalar
            def _(s):
                s.wait_ge(sV, v_var)
                s.activation(svar, svar, AF.Ln).then_inc(sS, 1)

            v_half = res("V")

            @blk.vector
            def _(v):
                v.wait_ge(sS, s_ln)
                v.tensor_scalar_mul(svar, svar, -0.5).then_inc(sV, 1)

            s_rstd = res("S")

            @blk.scalar
            def _(s):
                s.wait_ge(sV, v_half)
                s.activation(srstd, svar, AF.Exp).then_inc(sS, 1)

            vn = res("V", 5)

            @blk.vector
            def _(v):
                v.wait_ge(sS, s_rstd)
                v.tensor_scalar_mul(smean, smean, -1.0).then_inc(sV, 1)
                v.wait_ge(sV, vn[0])
                for grp in range(4):
                    ins = None
                    for c in range(4 * grp, 4 * grp + 4):
                        ins = v.tensor_scalar(dst[:, c, :], src[:, c, :],
                                              smean[:, c:c + 1], srstd[:, c:c + 1],
                                              op0=ALU.add, op1=ALU.mult)
                    ins.then_inc(sV, 1)
            return vn[1:]

        def transpose_to_feat(src_tok, dst_feat, ready_v):
            pT = [pB[:, 1024:1536].bitcast(BF16), pB[:, 1536:2048].bitcast(BF16)]
            n = NTC * NDC
            tr_ms = res("T", n)
            ev_ms = res("V", n)
            rv = ready_v if isinstance(ready_v, list) else [ready_v] * 4

            @blk.tensor
            def _(t):
                last = -1
                for idx in range(n):
                    c, k = divmod(idx, NDC)
                    if c // 4 != last:
                        last = c // 4
                        t.wait_ge(sV, rv[last])
                    if idx >= 2:
                        t.wait_ge(sV, ev_ms[idx - 2])
                    t.transpose(pT[idx % 2][:, 0:128],
                                src_tok[:, c, 128 * k:128 * (k + 1)],
                                I128).then_inc(sT, 1)

            @blk.vector
            def _(v):
                for idx in range(n):
                    c, k = divmod(idx, NDC)
                    v.wait_ge(sT, tr_ms[idx])
                    v.tensor_copy(dst_feat[:, k, 128 * c:128 * (c + 1)],
                                  pT[idx % 2][:, 0:128]).then_inc(sV, 1)
            return ev_ms[-1]

        def proj_feat(ws_ap, ws_dw, dst, bias_idx, extra_t_waits=(), mmaj=False):
            mm_ms = res("T", NDC)
            ev_ms = res("S", NDC)

            @blk.tensor
            def _(t):
                t.wait_ge(sV, v_xnf[0])
                t.wait_ge(sDW, ws_dw)
                for sem, val in extra_t_waits:
                    t.wait_ge(sem, val)
                for oc in range(NDC):
                    pm = pA if (oc % 2 == 0) else pB
                    if oc >= 2:
                        t.wait_ge(sS, ev_ms[oc - 2])
                    ins = None
                    for ic in range(NDC):
                        xm = xnf[:, ic, :].rearrange("p (a m) -> p m a", m=16) \
                            if mmaj else None
                        for nb in range(4):
                            rhs = (xm[:, 4 * nb:4 * (nb + 1), :] if mmaj
                                   else xnf[:, ic, 512 * nb:512 * (nb + 1)])
                            ins = t.matmul(pm[:, 512 * nb:512 * (nb + 1)],
                                           ws_ap[:, ic, 128 * oc:128 * (oc + 1)],
                                           rhs,
                                           start=(ic == 0), stop=(ic == NDC - 1))
                    ins.then_inc(sT, 1)

            @blk.scalar
            def _(s):
                s.wait_ge(sDO, do_smalls)
                for oc in range(NDC):
                    pm = pA if (oc % 2 == 0) else pB
                    s.wait_ge(sT, mm_ms[oc])
                    ins = None
                    for nb in range(4):
                        ins = s.activation(dst[:, oc, 512 * nb:512 * (nb + 1)],
                                           pm[:, 512 * nb:512 * (nb + 1)], AF.Gelu,
                                           bias=bias4[:, 8 * bias_idx + oc:8 * bias_idx + oc + 1])
                    ins.then_inc(sS, 1)
            return mm_ms[-1], ev_ms[-1]

        def proj_tok_softplus(ws_ap, ws_dw, src_feat, src_ready_s, rb_idx, dst):
            mm_ms = res("T", NTC)
            ev3 = res("S", 3 * NTC)
            ev_ms = ev3[2::3]

            @blk.tensor
            def _(t):
                t.wait_ge(sS, src_ready_s)
                t.wait_ge(sDW, ws_dw)
                t.wait_ge(sDW, dw_pre[0])
                for tc in range(NTC):
                    pm = pA if (tc % 2 == 0) else pB
                    if tc >= 2:
                        t.wait_ge(sS, ev_ms[tc - 2])
                    for ic in range(NDC):
                        for nh in range(2):
                            t.matmul(pm[:, 512 * nh:512 * (nh + 1)],
                                     src_feat[:, ic, 128 * tc:128 * (tc + 1)],
                                     ws_ap[:, ic, 512 * nh:512 * (nh + 1)],
                                     start=(ic == 0), stop=False)
                    ins = None
                    for nh in range(2):
                        ins = t.matmul(pm[:, 512 * nh:512 * (nh + 1)], ones1,
                                       rowb3[0:1, D * rb_idx + 512 * nh:D * rb_idx + 512 * (nh + 1)],
                                       start=False, stop=True)
                    ins.then_inc(sT, 1)

            @blk.scalar
            def _(s):
                s.wait_ge(sV, v_const)
                for tc in range(NTC):
                    pm = pA if (tc % 2 == 0) else pB
                    s.wait_ge(sT, mm_ms[tc])
                    for nh in range(2):
                        s.activation(pm[:, 1024 + 512 * nh:1024 + 512 * (nh + 1)],
                                     pm[:, 512 * nh:512 * (nh + 1)],
                                     AF.Exp).then_inc(sS, 1)
                        s.wait_ge(sS, ev3[3 * tc + nh])
                        ins = s.activation(dst[:, tc, 512 * nh:512 * (nh + 1)],
                                           pm[:, 1024 + 512 * nh:1024 + 512 * (nh + 1)],
                                           AF.Ln, bias=onef)
                    ins.then_inc(sS, 1)
            return mm_ms[-1], ev_ms[-1]

        def prefetch(slot_i, name, gate_T):
            dw = None

            @blk.gpsimd
            def _(g):
                if gate_T:
                    g.wait_ge(sT, gate_T)
                g.dma_start(out=wslot[slot_i][:, :, :],
                            in_=w_h[name].ap().rearrange("(ic p) o -> p ic o", p=128)
                            ).then_inc(sDW, 16)
            return res16("DW")

        tapx = [0]

        @blk.gpsimd
        def _(g):
            pass
        v_ln1g = layer_norm(xt, xn_tok, [],
                            per_chunk=(sDW, [dw_xchunks[c // 4] for c in range(NTC)]))
        v_ln1 = v_ln1g[-1]
        tap("xt", sDW, dw_x[0], A[:, P0:P0 + 16384])
        tap("xn", sV, v_ln1, A[:, P1:P1 + 16384])
        tap("stats", sV, v_ln1, A[:, E_ss:E_ss + 160].bitcast(F32))
        v_xnf = [0]
        v_xnf[0] = transpose_to_feat(xn_tok, xnf, v_ln1g)

        tap("xnf", sV, v_xnf[0], A[:, P0:P0 + 16384])
        dw_wk1 = prefetch(1, "wk1t", 0)
        q1_mm, q1_ev = proj_feat(wslot[0], dw_wq1[0], q1h, 0)
        tap("q1h", sS, q1_ev, A[:, P2:P2 + 16384])
        dw_wq2 = prefetch(0, "wq2t", q1_mm)
        k1_mm, k1_ev = proj_feat(wslot[1], dw_wk1, k1h, 1)
        dw_wk2 = prefetch(1, "wk2t", k1_mm)
        q2_mm, q2_ev = proj_tok_softplus(wslot[0], dw_wq2, q1h, q1_ev, 0, q_tok)
        tap("qtok", sS, q2_ev, A[:, P1:P1 + 16384])
        dw_wv = prefetch(0, "wvt", q2_mm)
        k2_mm, k2_ev = proj_tok_softplus(wslot[1], dw_wk2, k1h, k1_ev, 1, k_tok)
        tap("ktok", sS, k2_ev, A[:, P4:P4 + 16384])
        dw_wg = prefetch(1, "wgt", k2_mm)

        @blk.gpsimd
        def _(g):
            for i in range(4):
                g.dma_start(out=w1b.ap()[:, 11 * i:11 * (i + 1) if i < 3 else NR],
                            in_=w1p_h.ap()[:, 11 * i:11 * (i + 1) if i < 3 else NR]
                            ).then_inc(sDW, 16)
                res16("DW")
                g.dma_start(out=w2b.ap()[:, 11 * i:11 * (i + 1) if i < 3 else NR],
                            in_=w2p_h.ap()[:, 11 * i:11 * (i + 1) if i < 3 else NR]
                            ).then_inc(sDW, 16)
                res16("DW")
        dw_wcast = ct["DW"]


        a_mm = res("T")

        @blk.tensor
        def _(t):
            t.wait_ge(sS, q2_ev)
            t.wait_ge(sS, k2_ev)
            ins = None
            for p in range(8):
                for tc in range(NTC):
                    ins = t.matmul(pA[:, 128 * p:128 * (p + 1)],
                                   k_tok[:, tc, 128 * p:128 * (p + 1)],
                                   q_tok[:, tc, 128 * p:128 * (p + 1)],
                                   start=(tc == 0), stop=(tc == NTC - 1))
            ins.then_inc(sT, 1)

        v_aloc = res("V")

        @blk.vector
        def _(v):
            v.wait_ge(sT, a_mm)
            v.tensor_copy(aloc, pA[:, 0:1024]).then_inc(sV, 1)

        do_adram = res16("DO")

        @blk.sync
        def _(sy):
            sy.wait_ge(sV, v_aloc)
            sy.dma_start(out=a_dram.ap(), in_=aloc).then_inc(sDO, 16)

        tap("aloc", sV, v_aloc, aloc)
        xrr = xres_h.ap().rearrange("(c p) d -> p c d", p=128)

        @blk.gpsimd
        def _(g):
            g.wait_ge(sDO, do_adram)
            g.collective_compute(
                "AllReduce", ALU.add, replica_groups=[list(range(NCORE))],
                ins=[a_dram.ap()], outs=[a_red.ap()]).then_inc(sCC, 1)

        @blk.gpsimd
        def _(g):
            g.wait_ge(sT, a_mm)
            for i in range(4):
                g.dma_start(out=x2[:, 4 * i:4 * (i + 1), :],
                            in_=xrr[:, 4 * i:4 * (i + 1), :]).then_inc(sDW, 16)
                res16("DW")
        dw_xrl = ct["DW"]

        @blk.sync
        def _(sy):
            sy.wait_ge(sCC, 1)
            sy.dma_start(out=aloc, in_=a_red.ap()).then_inc(sAR, 16)

        tap("ared", sAR, 16, aloc)
        v_mm, v_ev = proj_feat(wslot[0], dw_wv, vfe, 2,
                               extra_t_waits=[(sV, v_aloc)], mmaj=True)
        dw_wp = prefetch(0, "wpt", v_mm)
        g_mm, g_ev = proj_feat(wslot[1], dw_wg, gfe, 3, mmaj=True)

        vbd2 = res("V", 2)
        v_bd = vbd2[1]

        @blk.vector
        def _(v):
            v.memset(A[:, E_BD:E_BD + 1024], 0.0).then_inc(sV, 1)
            v.wait_ge(sV, vbd2[0])
            v.wait_ge(sAR, 16)
            ar = aloc.rearrange("p (a b) -> p a b", b=128)
            ins = None
            for p in range(8):
                v.tensor_scalar_mul(BD[0:64, p, 0:64], ar[0:64, p, 0:64], SCALE)
                ins = v.tensor_scalar_mul(BD[64:128, p, 64:128],
                                          ar[64:128, p, 64:128], SCALE)
            ins.then_inc(sV, 1)

        tap("bd", sV, v_bd, A[:, E_BD:E_BD + 1024])
        tap("vfe", sS, v_ev, A[:, P2:P2 + 16384])
        tap("gfe", sS, g_ev, A[:, P3:P3 + 16384])
        z_mm = res("T")

        @blk.tensor
        def _(t):
            t.wait_ge(sV, v_bd)
            t.wait_ge(sS, g_ev)
            ins = None
            for p in range(8):
                ins = t.matmul(pB[:, p:p + 1], BD[:, p, :], ones128,
                               start=True, stop=True)
            ins.then_inc(sT, 1)

        vz2 = res("V", 2)
        v_z = vz2[1]

        @blk.vector
        def _(v):
            v.wait_ge(sT, z_mm)
            v.tensor_scalar_add(zsb, pB[:, 0:8], 1024.0).then_inc(sV, 1)
            v.wait_ge(sV, vz2[0])
            v.reciprocal(zsb, zsb).then_inc(sV, 1)

        tap("z", sV, v_z, zsb)
        o_mm = res("T", NDC)
        o_ev = res("V", NDC)

        @blk.tensor
        def _(t):
            t.wait_ge(sV, v_z)
            for p in range(NDC):
                pm = pA if (p % 2 == 0) else pB
                if p >= 2:
                    t.wait_ge(sV, o_ev[p - 2])
                ins = None
                for nb in range(4):
                    t.matmul(pm[:, 512 * nb:512 * (nb + 1)], BD[:, p, :],
                             vfe[:, p, 512 * nb:512 * (nb + 1)],
                             start=True, stop=False)
                    ins = t.matmul(pm[:, 512 * nb:512 * (nb + 1)], I128,
                                   vfe[:, p, 512 * nb:512 * (nb + 1)],
                                   start=False, stop=True)
                ins.then_inc(sT, 1)

        @blk.vector
        def _(v):
            for p in range(NDC):
                pm = pA if (p % 2 == 0) else pB
                v.wait_ge(sT, o_mm[p])
                ins = None
                for nb in range(4):
                    ins = v.scalar_tensor_tensor(
                        Opp[:, p, 512 * nb:512 * (nb + 1)],
                        pm[:, 512 * nb:512 * (nb + 1)],
                        zsb[:, p:p + 1],
                        gfe[:, p, 512 * nb:512 * (nb + 1)],
                        op0=ALU.mult, op1=ALU.mult)
                ins.then_inc(sV, 1)

        tap("opp", sV, o_ev[-1], A[:, P0:P0 + 16384])
        o2s_wr = [0] * NDC

        @blk.gpsimd
        def _(sy):
            for p in range(NDC):
                sy.wait_ge(sV, o_ev[p])
                for h2 in range(2):
                    o2s_out = bass.AP(o2s.ap().tensor, (2 * p + h2) * 131072,
                                      [[128, 64], [16384, 8], [8192, 2], [1, 128]])
                    sy.dma_start(
                        out=o2s_out,
                        in_=Opp[64 * h2:64 * (h2 + 1), p, :]
                        .rearrange("p (c t a) -> p c t a", c=8, t=2)
                    ).then_inc(sO2S, 16)
                o2s_wr[p] = 16 * 2 * (p + 1)

        sfeat = v3(P0, NDC, TOK)
        do_sf = [0]

        @blk.sync
        def _(sy):
            sy.wait_ge(sO2S, o2s_wr[-1])
            for ci in range(8):
                sf_in = bass.AP(
                    o2s.ap().tensor, 16384 * ci,
                    [[128, 128], [131072, 16], [1, 128]],
                )
                sy.dma_start(out=sfeat[:, ci, :], in_=sf_in).then_inc(sSF, 16)
            do_sf[0] = 128

        p_mm = res("T", NTC)
        p_ev = res("V", NTC)

        @blk.tensor
        def _(t):
            t.wait_ge(sSF, do_sf[0])
            t.wait_ge(sDW, dw_wp)
            t.wait_ge(sDW, dw_xrl)
            for tc in range(NTC):
                pm = pA if (tc % 2 == 0) else pB
                if tc >= 2:
                    t.wait_ge(sV, p_ev[tc - 2])
                for ic in range(NDC):
                    for nh in range(2):
                        t.matmul(pm[:, 512 * nh:512 * (nh + 1)],
                                 sfeat[:, ic, 128 * tc:128 * (tc + 1)],
                                 wslot[0][:, ic, 512 * nh:512 * (nh + 1)],
                                 start=(ic == 0), stop=False)
                ins = None
                for nh in range(2):
                    ins = t.matmul(pm[:, 512 * nh:512 * (nh + 1)], ones1,
                                   rowb3[0:1, 2 * D + 512 * nh:2 * D + 512 * (nh + 1)],
                                   start=False, stop=True)
                ins.then_inc(sT, 1)

        @blk.vector
        def _(v):
            for tc in range(NTC):
                pm = pA if (tc % 2 == 0) else pB
                v.wait_ge(sT, p_mm[tc])
                v.scalar_tensor_tensor(x2[:, tc, :], pm[:, 0:1024], 0.0,
                                       x2[:, tc, :], op0=ALU.bypass,
                                       op1=ALU.add).then_inc(sV, 1)
        v_x2 = p_ev[-1]

        if STAGE == "attn":
            @blk.gpsimd
            def _(g):
                g.wait_ge(sV, v_x2)
                for c in range(NTC):
                    g.dma_start(out=xo[:, c, :], in_=x2[:, c, :]).then_inc(sDW, 16)
                    res16("DW")
                g.wait_ge(sDW, ct["DW"])
            return nc

        def _dump_x2_and_end(wait_list):
            @blk.gpsimd
            def _(g):
                for sem, val in wait_list:
                    g.wait_ge(sem, val)
                for c in range(NTC):
                    g.dma_start(out=xo[:, c, :], in_=x2[:, c, :]).then_inc(sDW, 16)
                    res16("DW")
                g.wait_ge(sDW, ct["DW"])

        v_ln2g = layer_norm(x2, yn_tok, [], per_chunk=(sV, p_ev))
        v_ln2 = v_ln2g[-1]
        v_ynf = transpose_to_feat(yn_tok, ynf, v_ln2g)
        if LEVEL <= 1:
            _dump_x2_and_end([(sV, v_ynf)])
            return nc

        @blk.gpsimd
        def _(g):
            g.wait_ge(sT, p_mm[-1])
            g.dma_start(out=wcsb[:, :, :],
                        in_=wc_h.ap().rearrange("(ic p) e -> p ic e", p=128)
                        ).then_inc(sDW, 16)
        dw_wc = res16("DW")

        rt_mm = res("T")

        @blk.tensor
        def _(t):
            t.wait_ge(sV, v_ynf)
            t.wait_ge(sDW, dw_wc)
            ins = None
            for ic in range(NDC):
                for nb in range(4):
                    ins = t.matmul(pA[:, 512 * nb:512 * (nb + 1)], wcsb[:, ic, :],
                                   ynf[:, ic, 512 * nb:512 * (nb + 1)],
                                   start=(ic == 0), stop=(ic == NDC - 1))
            ins.then_inc(sT, 1)

        vp3 = res("V", 3)
        v_perm = vp3[2]

        @blk.vector
        def _(v):
            v.wait_ge(sT, rt_mm)
            v.wait_ge(sDO, do_smalls)
            lg = pA[:, 0:2048].rearrange("p (g w) -> p g w", w=32)
            v.scalar_tensor_tensor(lg, lg, bcv[:, 0:1],
                                   tie.unsqueeze(1).broadcast_to([128, NG, 32]),
                                   op0=ALU.add, op1=ALU.add).then_inc(sV, 1)
            v.wait_ge(sV, vp3[0])
            v.reduce_max(gmax.unsqueeze(2), lg, axis=AX.X).then_inc(sV, 1)
            v.wait_ge(sV, vp3[1])
            v.tensor_tensor(perm.rearrange("p (g w) -> p g w", w=32), lg,
                            gmax.unsqueeze(2).broadcast_to([128, NG, 32]),
                            op=ALU.is_equal).then_inc(sV, 1)

        if LEVEL <= 2:
            _dump_x2_and_end([(sV, v_perm)])
            return nc
        tap("yn", sV, v_ln2, A[:, P4:P4 + 16384])
        tap("perm", sV, v_perm, A[:, P0 + F_PERM:P0 + F_PERM + 2048])
        tap("gmax", sV, v_perm, A[:, E_gmax:E_gmax + 128].bitcast(F32))
        pt_ms = res("T", NTC)
        ptv_ms = res("V", NTC)
        pTa = pB[:, 1024:1536].bitcast(BF16)
        pTb = pB[:, 1536:2048].bitcast(BF16)

        @blk.tensor
        def _(t):
            t.wait_ge(sV, v_perm)
            for c in range(NTC):
                if c >= 2:
                    t.wait_ge(sV, ptv_ms[c - 2])
                t.transpose((pTa if c % 2 == 0 else pTb)[:, 0:128],
                            perm[:, 128 * c:128 * (c + 1)], I128).then_inc(sT, 1)

        @blk.vector
        def _(v):
            for c in range(NTC):
                v.wait_ge(sT, pt_ms[c])
                v.tensor_copy(permT[:, c, :],
                              (pTa if c % 2 == 0 else pTb)[:, 0:128]).then_inc(sV, 1)
        v_permt = ptv_ms[-1]

        if LEVEL <= 3:
            _dump_x2_and_end([(sV, v_permt)])
            return nc
        NHALF = 2 if LEVEL >= 10 else 1
        wblk_dw_all = []
        blk_gate = []

        half_res = []
        for h in range(NHALF):
            hr = {}
            hr["v_pbig2"] = res("V", 2)
            hr["v_pbig"] = hr["v_pbig2"][1]
            hr["gx_ms"] = res("T", 64)
            hr["gx_ev"] = res("V", 64)
            hr["w1_ms"] = []
            hr["w2_ms"] = []
            for r in range(NR):
                hr["w1_ms"].append(res("T"))
                hr["w2_ms"].append(res("T"))
            hr["in_ev"] = [res("S") for r in range(NR)]
            hr["r3_ms"] = [res("V") for r in range(NR)]
            hr["sc_ms"] = res("T", 8)
            hr["sc_ev"] = res("V", 8)
            half_res.append(hr)
        glb = []
        for h in range(NHALF):
            for (rb0, rbn) in BLOCKS:
                glb.append(half_res[h]["w2_ms"][rb0 + rbn - 1])

        wdma_ms = []

        @blk.gpsimd
        def _(g):
            if LEVEL <= 4:
                return
            gi = 0
            for h in range(NHALF):
                for bi, (rb0, rbn) in enumerate(BLOCKS):
                    sl = bi % 2
                    if gi >= 2 and LEVEL >= 6:
                        g.wait_ge(sT, glb[gi - 2])
                    else:
                        g.wait_ge(sT, rt_mm)
                    ws = sWS[gi % 2]
                    wsk = f"WS{gi % 2}"
                    if gi == 0:
                        g.wait_ge(sDW, dw_wcast)
                    g.dma_start(out=A[:, P5 + F_W1[sl]:P5 + F_W1[sl] + rbn * 768],
                                in_=w1b.ap()[:, rb0:rb0 + rbn, :, :, :]
                                .rearrange("p r q k f -> p (r q k f)")
                                ).then_inc(ws, 16)
                    xsem[wsk] += 16
                    g.dma_start(out=A[:, P5 + F_W2[sl]:P5 + F_W2[sl] + rbn * 1024],
                                in_=w2b.ap()[:, rb0:rb0 + rbn, :]
                                .rearrange("p r d -> p (r d)")).then_inc(ws, 16)
                    xsem[wsk] += 16
                    wdma_ms.append((ws, xsem[wsk]))
                    gi += 1

        for h in range(NHALF):
            hr = half_res[h]
            hr["r3_dma"] = [None] * NR
            hr["r2_dma"] = [None] * 8
            hr["out_dma"] = [None] * 8

        for h in range(NHALF):
            hr = half_res[h]
            prev = half_res[h - 1] if h == 1 else None

            @blk.vector
            def _(v, h=h, hr=hr, prev=prev):
                v.wait_ge(sV, v_permt)
                if prev is not None:
                    v.wait_ge(sT, prev["sc_ms"][-1])
                    v.wait_ge(*prev["out_dma"][-1])
                if h == 0:
                    v.memset(A[:, P5 + 11264:P5 + 11264 + 192], 0.0)
                v.memset(A[:, P0 + F_PBIG:P0 + F_PBIG + 4096], 0.0).then_inc(sV, 1)
                v.wait_ge(sV, hr["v_pbig2"][0])
                ins = None
                for c8 in range(8):
                    c = 8 * h + c8
                    for j in range(4):
                        ins = v.tensor_copy(
                            pbig[32 * j:32 * (j + 1), c8, 128 * j:128 * (j + 1)],
                            permT[32 * j:32 * (j + 1), c, :])
                ins.then_inc(sV, 1)

            @blk.tensor
            def _(t, h=h, hr=hr):
                t.wait_ge(sV, hr["v_pbig"])
                for idx in range(64):
                    c8, k = divmod(idx, 8)
                    c = 8 * h + c8
                    sl = idx % 2
                    if idx >= 2:
                        t.wait_ge(sV, hr["gx_ev"][idx - 2])
                    t.matmul(pB[:, 512 * sl:512 * (sl + 1)],
                             yn_tok[:, c, 128 * k:128 * (k + 1)],
                             pbig[:, c8, :], start=True, stop=True).then_inc(sT, 1)

            @blk.vector
            def _(v, h=h, hr=hr):
                for idx in range(64):
                    c8, k = divmod(idx, 8)
                    sl = idx % 2
                    v.wait_ge(sT, hr["gx_ms"][idx])
                    base = (P2 if k < 4 else P3) + (k % 4) * 4096 + 512 * c8
                    v.tensor_copy(A[:, base:base + 512],
                                  pB[:, 512 * sl:512 * (sl + 1)]).then_inc(sV, 1)

            if h == 0:
                tap("xlo", sV, hr["gx_ev"][-1], A[:, P2:P2 + 16384], eng="sync")
                tap("xhi", sV, hr["gx_ev"][-1], A[:, P3:P3 + 16384], eng="sync")
            if LEVEL <= 4:
                _dump_x2_and_end([(sV, hr["gx_ev"][-1])])
                return nc
            if LEVEL == 5:
                _dump_x2_and_end([(sV, hr["gx_ev"][-1]), (sDW, wdma_ms[-1])])
                return nc
            PINS = [(pA, 0), (pA, 512), (pB, 0)]

            @blk.tensor
            def _(t, h=h, hr=hr, PINS=PINS):
                t.wait_ge(sV, hr["gx_ev"][-1])
                for bi, (rb0, rbn) in enumerate(BLOCKS):
                    t.wait_ge(*wdma_ms[h * NB + bi])
                    sl = bi % 2
                    for rr in range(rbn):
                        r = rb0 + rr
                        nq = 3 if r < NR - 1 else 2
                        rdb = r % 2
                        pw = (pA if rdb == 0 else pB)[:, 1024:2048]
                        r4 = r % 4
                        if r >= 1:
                            t.wait_ge(sS, hr["in_ev"][r - 1])
                        ins = None
                        for k in range(8):
                            for q in range(nq):
                                es = 3 * r + q
                                pin, pco = PINS[q]
                                ins = t.matmul(pin[32 * q:32 * (q + 1), pco:pco + 32],
                                               w1blk[sl][:, rr, q, k, :],
                                               x_ap(h, es, k),
                                               start=(k == 0), stop=(k == 7),
                                               tile_position=(0, 32 * q))
                        ins.then_inc(sT, 1)
                        t.wait_ge(sS, hr["in_ev"][r])
                        if LEVEL >= 7:
                            if r >= 2:
                                t.wait_ge(sV, hr["r3_ms"][r - 2])
                            ins = None
                            for nh in range(2):
                                ins = t.matmul(pw[0:96, 512 * nh:512 * (nh + 1)],
                                               bdin[rdb][0:96, :],
                                               w2blk[sl][0:96, rr,
                                                         512 * nh:512 * (nh + 1)],
                                               start=True, stop=True)
                            ins.then_inc(sT, 1)
                        else:
                            t.nop().then_inc(sT, 1)

            @blk.scalar
            def _(s_, h=h, hr=hr, PINS=PINS):
                for r in range(NR):
                    nq = 3 if r < NR - 1 else 2
                    rdb = r % 2
                    r4 = r % 4
                    pw = (pA if rdb == 0 else pB)[:, 1024:2048]
                    s_.wait_ge(sT, hr["w1_ms"][r])
                    ins = None
                    for q in range(nq):
                        pin, pco = PINS[q]
                        ins = s_.activation(bdin[rdb][32 * q:32 * (q + 1),
                                                      32 * q:32 * (q + 1)],
                                            pin[32 * q:32 * (q + 1), pco:pco + 32],
                                            AF.Relu,
                                            bias=b1p[32 * q:32 * (q + 1), r:r + 1])
                    ins.then_inc(sS, 1)


            @blk.vector
            def _(v, h=h, hr=hr):
                for r in range(NR):
                    rdb = r % 2
                    r4 = r % 4
                    pw = (pA if rdb == 0 else pB)[:, 1024:2048]
                    if LEVEL >= 7:
                        v.wait_ge(sT, hr["w2_ms"][r])
                        if LEVEL >= 8 and r >= 4:
                            gr2 = h * NR + r - 4
                            v.wait_ge(sR3[gr2 % 3], 16 * (gr2 // 3 + 1))
                        v.tensor_copy(r3[r4][0:96, :], pw[0:96, 0:1024]).then_inc(sV, 1)
                    else:
                        v.nop().then_inc(sV, 1)

            @blk.sync
            def _(sy, h=h, hr=hr):
                if LEVEL < 8:
                    return
                for r in range(NR):
                    nq = 3 if r < NR - 1 else 2
                    r4 = r % 4
                    gr = h * NR + r
                    sy.wait_ge(sV, hr["r3_ms"][r])
                    sem = sR3[gr % 3]
                    xsem[f"R3{gr % 3}"] += 16
                    hr["r3_dma"][r] = (sem, xsem[f"R3{gr % 3}"])
                    sy.dma_start(
                        out=r2d.ap()[3 * r:3 * r + nq, 32 * h:32 * (h + 1), :],
                        in_=r3[r4][0:32 * nq, :].rearrange(
                            "p (q d) -> p q d", q=1)).then_inc(sem, 16)

            if h == 0:
                if TAP == "r2d":
                    @blk.sync
                    def _(sy, hr=hr):
                        sy.wait_ge(sDO, hr["r3_dma"][-1])
                        sy.dma_start(out=tap2_h.ap()[:, 0:8192],
                                     in_=r2d.ap()[:, 0:8, :].rearrange(
                                         "e g d -> e (g d)")).then_inc(sDO, 16)
                        res16("DO")
            if LEVEL in (6, 7):
                _dump_x2_and_end([(sS, hr["in_ev"][-1]), (sV, hr["r3_ms"][-1])])
                return nc
            if LEVEL == 8:
                _dump_x2_and_end([(sDO, hr["r3_dma"][-1])])
                return nc
            r3fin = [(sR3[i], xsem[f"R3{i}"]) for i in range(3)]

            @blk.scalar
            def _(sy, h=h, hr=hr, r3fin=r3fin):
                for c8 in range(8):
                    c = 8 * h + c8
                    sl = c8 % 4
                    gc = 8 * h + c8
                    if c8 == 0:
                        for sem, val in r3fin:
                            sy.wait_ge(sem, val)
                    if c8 >= 4:
                        sy.wait_ge(sT, hr["sc_ms"][c8 - 4])
                    sem = sR2[gc % 2]
                    xsem[f"R2{gc % 2}"] += 16
                    hr["r2_dma"][c8] = (sem, xsem[f"R2{gc % 2}"])
                    sy.dma_start(out=r2blk[sl][:, :, :],
                                 in_=r2d.ap()[:, 4 * c:4 * (c + 1), :]
                                 ).then_inc(sem, 16)

            @blk.sync
            def _(sy, h=h, hr=hr):
                for c8 in range(8):
                    cc = 8 * h + c8
                    sy.wait_ge(sV, hr["sc_ev"][c8])
                    sem = sOD[cc % 2]
                    xsem[f"OD{cc % 2}"] += 16
                    hr["out_dma"][c8] = (sem, xsem[f"OD{cc % 2}"])
                    sy.dma_start(out=xo[:, cc, :],
                                 in_=outst[c8 % 4]).then_inc(sem, 16)

            @blk.tensor
            def _(t, h=h, hr=hr):
                for c8 in range(8):
                    sl = c8 % 4
                    pm = pA if c8 % 2 == 0 else pB
                    t.wait_ge(*hr["r2_dma"][c8])
                    if c8 >= 2:
                        t.wait_ge(sV, hr["sc_ev"][c8 - 2])
                    c = 8 * h + c8
                    ins = None
                    for j in range(4):
                        g = 4 * c + j
                        for nh in range(2):
                            t.matmul(pm[32 * j:32 * (j + 1), 512 * nh:512 * (nh + 1)],
                                     perm[:, 32 * g:32 * (g + 1)],
                                     b2sb[:, 512 * nh:512 * (nh + 1)],
                                     start=True, stop=False,
                                     tile_position=(0, 32 * j))
                            ins = t.matmul(pm[32 * j:32 * (j + 1), 512 * nh:512 * (nh + 1)],
                                           perm[:, 32 * g:32 * (g + 1)],
                                           r2blk[sl][:, j, 512 * nh:512 * (nh + 1)],
                                           start=False, stop=True,
                                           tile_position=(0, 32 * j))
                    ins.then_inc(sT, 1)

            @blk.vector
            def _(v, h=h, hr=hr):
                for c8 in range(8):
                    c = 8 * h + c8
                    sl = c8 % 2
                    pm = pA if c8 % 2 == 0 else pB
                    v.wait_ge(sT, hr["sc_ms"][c8])
                    if c8 >= 4:
                        v.wait_ge(*hr["out_dma"][c8 - 4])
                    v.scalar_tensor_tensor(outst[c8 % 4], pm[:, 0:1024], 0.0,
                                           x2[:, c, :], op0=ALU.bypass,
                                           op1=ALU.add).then_inc(sV, 1)



        @blk.sync
        def _(sy):
            sy.wait_ge(sDO, ct["DO"])
            for i in range(2):
                sy.wait_ge(sOD[i], xsem[f"OD{i}"])
                sy.wait_ge(sR2[i], xsem[f"R2{i}"])
                sy.wait_ge(sWS[i], xsem[f"WS{i}"])
            for i in range(3):
                sy.wait_ge(sR3[i], xsem[f"R3{i}"])
    return nc


def _prep(inputs):
    g = {k: np.asarray(v, dtype=np.float32) for k, v in inputs.items()}
    x = g["L0_attn_x"].reshape(16384, 1024)
    wT = lambda w: np.ascontiguousarray(w.T)
    com = {
        "wq1t": wT(g["L0_attn_q_w1"]), "wk1t": wT(g["L0_attn_k_w1"]),
        "wq2t": wT(g["L0_attn_q_w2"]), "wk2t": wT(g["L0_attn_k_w2"]),
        "wvt": wT(g["L0_attn_v_w"]), "wgt": wT(g["L0_attn_g_w"]),
        "wpt": wT(g["L0_attn_p_w"]),
    }
    bias4 = np.stack([g["L0_attn_q_b1"].reshape(8, 128).T,
                      g["L0_attn_k_b1"].reshape(8, 128).T,
                      g["L0_attn_v_b"].reshape(8, 128).T,
                      g["L0_attn_g_b"].reshape(8, 128).T], axis=1)
    com["bias4"] = np.ascontiguousarray(bias4.reshape(128, 32))
    com["rowb3"] = np.ascontiguousarray(
        np.concatenate([g["L0_attn_q_b2"], g["L0_attn_k_b2"],
                        g["L0_attn_p_b"]]).reshape(1, 3072))
    com["wc"] = np.ascontiguousarray(g["L0_ffn_Wc"].reshape(1024, 128))
    com["bcv"] = np.ascontiguousarray(g["L0_ffn_bc"].reshape(128, 1))
    b1es = g["L0_ffn_b1"].reshape(128, 32)
    b1p = np.zeros((128, NR), np.float32)
    w1es = g["L0_ffn_W1"].reshape(128, 1024, 32)
    w1p = np.zeros((128, NR, 3, 8, 32), np.float32)
    w2es = g["L0_ffn_W2"].reshape(128, 1024, 32)
    w2p = np.zeros((128, NR, 1024), np.float32)
    for q in range(3):
        idx = np.arange(q, 128, 3)
        b1p[32 * q:32 * q + 32, :len(idx)] = b1es[idx].T
        w1p[:, :len(idx), q] = w1es[idx].reshape(len(idx), 8, 128, 32).transpose(2, 0, 1, 3)
        w2p[32 * q:32 * q + 32, :len(idx), :] = w2es[idx].transpose(2, 0, 1)
    com["b1p"] = b1p
    com["w1p"] = np.ascontiguousarray(w1p)
    com["w2p"] = np.ascontiguousarray(w2p)
    com["b2"] = np.ascontiguousarray(g["L0_ffn_b2"].reshape(128, 1024))
    in_maps = []
    for c in range(NCORE):
        m = dict(com)
        m["x"] = np.ascontiguousarray(x[c * TOK:(c + 1) * TOK])
        idx = (np.arange(16)[:, None] * 1024 + 128 * c
               + np.arange(128)[None, :]).ravel()
        m["xres"] = np.ascontiguousarray(x[idx])
        in_maps.append(m)
    return in_maps


TAP_OUT = None
TAP2_OUT = None


def kernel(**inputs):
    in_maps = _prep(inputs)
    nc = build_nc()
    res = run_bass_kernel_spmd(nc, in_maps, core_ids=list(range(NCORE)))
    if TAP:
        globals()["TAP_OUT"] = [np.asarray(r["tap"]) for r in res.results]
        globals()["TAP2_OUT"] = [np.asarray(r["tap2"]) for r in res.results]
    outp = np.empty((16384, 1024), np.float32)
    for c in range(NCORE):
        idx = (np.arange(16)[:, None] * 1024 + 128 * c
               + np.arange(128)[None, :]).ravel()
        outp[idx] = np.asarray(res.results[c]["out"])
    return outp.reshape(1, 16384, 1024)


# version 16
# speedup vs baseline: 1.1665x; 1.1665x over previous
import numpy as np
import ml_dtypes
from contextlib import ExitStack

import concourse.bass as bass
import concourse.mybir as mybir
from concourse.bass_utils import run_bass_kernel_spmd

F32 = mybir.dt.float32
BF16 = mybir.dt.bfloat16
AF = mybir.ActivationFunctionType
ALU = mybir.AluOpType
AX = mybir.AxisListType

NCORE = 8
TOK = 2048
NTC = 16
D = 1024
NDC = 8
SCALE = 0.125
NES = 128
NG = 64
NR = 43
RB = 2
EPS = 1e-5

STAGE = "full"
TAP = None
LEVEL = 10

AR = 103344
P0, P1, P2, P3, P4, P5 = 0, 16384, 32768, 49152, 65536, 81920
E_bias4, E_bc, E_b1p, E_tie = 98304, 98432, 98448, 98544
E_ss, E_sq, E_m, E_var, E_rstd = 98608, 98640, 98672, 98704, 98736
E_z, E_gmax, E_I128, E_ones128, E_ones1, E_zcol = 98768, 98784, 98912, 99040, 99056, 99042
E_BD, E_aloc, E_innerT, E_b2 = 99184, 100208, 102256, 102320
W_SLOT = [0, 8192]
F_W1 = [0, 1536]
F_W2 = [3072, 5120]
F_ROWB = 13312
F_WC = 13312
F_R3 = [7168, 8192, 9216, 10240]
F_PERM, F_PERMT, F_PBIG = 0, 2048, 4096
F_R2 = [0, 4096]
F_OUT = [0, 2048]

BLOCKS = []
_r0 = 0
while _r0 < NR:
    BLOCKS.append((_r0, min(RB, NR - _r0)))
    _r0 += RB
NB = len(BLOCKS)


def build_nc():
    nc = bass.Bass("TRN2", target_bir_lowering=False, debug=False,
                   detect_race_conditions=False)
    x_h = nc.dram_tensor("x", [TOK, D], F32, kind="ExternalInput")
    xres_h = nc.dram_tensor("xres", [TOK, D], F32, kind="ExternalInput")
    w_h = {}
    for nm in ["wq1t", "wk1t", "wq2t", "wk2t", "wvt", "wgt", "wpt"]:
        w_h[nm] = nc.dram_tensor(nm, [D, D], F32, kind="ExternalInput")
    bias4_h = nc.dram_tensor("bias4", [128, 32], F32, kind="ExternalInput")
    rowb3_h = nc.dram_tensor("rowb3", [1, 3 * D], F32, kind="ExternalInput")
    wc_h = nc.dram_tensor("wc", [D, NES], F32, kind="ExternalInput")
    bc_h = nc.dram_tensor("bcv", [128, 1], F32, kind="ExternalInput")
    b1p_h = nc.dram_tensor("b1p", [128, NR], F32, kind="ExternalInput")
    w1p_h = nc.dram_tensor("w1p", [128, NR, 3, 8, 32], F32, kind="ExternalInput")
    w2p_h = nc.dram_tensor("w2p", [128, NR, D], F32, kind="ExternalInput")
    b2_h = nc.dram_tensor("b2", [128, D], F32, kind="ExternalInput")
    out_h = nc.dram_tensor("out", [TOK, D], F32, kind="ExternalOutput")
    tap_h = nc.dram_tensor("tap", [128, 16384], F32, kind="ExternalOutput") if TAP else None
    tap2_h = nc.dram_tensor("tap2", [128, 16384], BF16, kind="ExternalOutput") if TAP else None
    a_dram = nc.dram_tensor("a_dram", [128, 1024], F32, kind="Internal")
    a_red = nc.dram_tensor("a_red", [128, 1024], F32, kind="Internal")
    r2d = nc.dram_tensor("r2d", [NES, NG, D], BF16, kind="Internal")
    o2s = nc.dram_tensor("o2s", [16, 8, 2, 64, 128], BF16, kind="Internal")
    w1b = nc.dram_tensor("w1b", [128, NR, 3, 8, 32], BF16, kind="Internal")
    w2b = nc.dram_tensor("w2b", [128, NR, D], BF16, kind="Internal")
    x = x_h.ap()
    out = out_h.ap()
    xo = out.rearrange("(c p) d -> p c d", p=128)
    xr = x.rearrange("(c p) d -> p c d", p=128)

    with ExitStack() as st:
        E = st.enter_context
        A = E(nc.sbuf_tensor("arena", [128, AR], BF16))
        rowb3_t = E(nc.sbuf_tensor("rowb3s", [1, 3 * D], BF16))
        pA = E(nc.psum_tensor("pA", [128, 2048], F32))
        pB = E(nc.psum_tensor("pB", [128, 2048], F32))
        sT = E(nc.semaphore("sT"))
        sV = E(nc.semaphore("sV"))
        sS = E(nc.semaphore("sS"))
        sP = E(nc.semaphore("sP"))
        sDW = E(nc.semaphore("sDW"))
        sDO = E(nc.semaphore("sDO"))
        sCC = E(nc.semaphore("sCC"))
        sR3 = [E(nc.semaphore(f"sR3{i}")) for i in range(3)]
        sR2 = [E(nc.semaphore(f"sR2{i}")) for i in range(2)]
        sOD = [E(nc.semaphore(f"sOD{i}")) for i in range(2)]
        sWS = [E(nc.semaphore(f"sWS{i}")) for i in range(2)]
        sO2S = E(nc.semaphore("sO2S"))
        sSF = E(nc.semaphore("sSF"))
        sAR = E(nc.semaphore("sAR"))
        xsem = {"R30": 0, "R31": 0, "R32": 0, "R20": 0, "R21": 0,
                "OD0": 0, "OD1": 0, "WS0": 0, "WS1": 0}
        blk = E(nc.Block())

        ct = {"T": 0, "V": 0, "S": 0, "P": 0, "DW": 0, "DO": 0}

        def res(key, n=1):
            v = [ct[key] + i + 1 for i in range(n)]
            ct[key] += n
            return v if n > 1 else v[0]

        def res16(key):
            ct[key] += 16
            return ct[key]

        def bf(off, n):
            return A[:, off:off + n]

        def f32v(off, nf):
            return A[:, off:off + 2 * nf].bitcast(F32)

        def v3(off, a, b_):
            return A[:, off:off + a * b_].rearrange("p (a b) -> p a b", b=b_)

        xt = v3(P0, NTC, D)
        xnf = v3(P0, NDC, TOK)
        Opp = v3(P0, NDC, TOK)
        ynf = v3(P0, NDC, TOK)
        xn_tok = v3(P1, NTC, D)
        q_tok = v3(P1, NTC, D)
        x2 = v3(P1, NTC, D)
        q1h = v3(P2, NDC, TOK)
        vfe = v3(P2, NDC, TOK)
        k1h = v3(P3, NDC, TOK)
        gfe = v3(P3, NDC, TOK)
        k_tok = v3(P4, NTC, D)
        yn_tok = v3(P4, NTC, D)
        wslot = [v3(P5 + W_SLOT[i], NDC, D) for i in range(2)]
        scr0 = bf(P2, 1024)
        icol = bf(P2 + 1024, 128)
        irow = bf(P2 + 1152, 128)
        I128 = bf(E_I128, 128)
        ones128 = bf(E_ones128, 1)
        ones1 = A[0:1, E_ones1:E_ones1 + 128]
        zcol = bf(E_zcol, 1)
        onef = A[:, 99044:99048].bitcast(F32)[:, 0:1]
        bias4 = f32v(E_bias4, 32)
        bcv = f32v(E_bc, 1)
        b1p = f32v(E_b1p, NR)
        tie = f32v(E_tie, 32)
        ssum = f32v(E_ss, 16)
        ssq = f32v(E_sq, 16)
        smean = f32v(E_m, 16)
        svar = f32v(E_var, 16)
        srstd = f32v(E_rstd, 16)
        zsb = f32v(E_z, 8)
        gmax = f32v(E_gmax, 64)
        BD = v3(E_BD, 8, 128)
        aloc = f32v(E_aloc, 1024)
        innerT = v3(E_innerT, 2, 32)
        b2sb = bf(E_b2, D)
        rowb3 = rowb3_t[0:1, :]
        wcsb = v3(P5 + F_WC, NDC, NES)
        perm = bf(P0 + F_PERM, TOK)
        permT = v3(P0 + F_PERMT, NTC, 128)
        pbig = v3(P0 + F_PBIG, 8, 512)
        r2blk = [v3(P2 + 4096 * i, 4, D) for i in range(4)]
        outst = [A[:, P3 + 2048 * i:P3 + 2048 * (i + 1)].bitcast(F32) for i in range(4)]
        r3 = [bf(P5 + F_R3[i], 1024) for i in range(4)]
        w1blk = [A[:, P5 + F_W1[i]:P5 + F_W1[i] + RB * 768].rearrange(
            "p (r q k f) -> p r q k f", q=3, k=8, f=32) for i in range(2)]
        w2blk = [v3(P5 + F_W2[i], RB, D) for i in range(2)]
        bdin = [A[:, P5 + 11264 + 96 * i:P5 + 11264 + 96 * (i + 1)] for i in range(2)]

        def x_ap(h, es, k):
            base = (P2 if k < 4 else P3) + (k % 4) * 4096
            return A[:, base:base + 4096].rearrange(
                "p (c j e) -> p c j e", j=4, e=128)[:, :, :, es]

        def tap(name, sem, val, ap, eng="gpsimd"):
            if TAP != name:
                return
            n = ap.shape[-1] if ap.ndim == 2 else int(np.prod(ap.shape[1:]))
            npart = ap.shape[0]
            if eng == "sync":
                @blk.sync
                def _(sy):
                    sy.wait_ge(sem, val)
                    sy.dma_start(out=tap2_h.ap()[0:npart, 0:n],
                                 in_=ap).then_inc(sDO, 16)
                    res16("DO")
            else:
                @blk.gpsimd
                def _(g):
                    g.wait_ge(sem, val)
                    g.dma_start(out=tap_h.ap()[0:npart, 0:n],
                                in_=ap).then_inc(sDW, 16)
                    res16("DW")

        dw_x = [0]
        dw_wq1 = [0]
        dw_pre = [0]
        p_const = res("P")

        dw_xchunks = []

        @blk.gpsimd
        def _(g):
            for i in range(4):
                g.dma_start(out=xt[:, 4 * i:4 * (i + 1), :],
                            in_=xr[:, 4 * i:4 * (i + 1), :]).then_inc(sDW, 16)
                dw_xchunks.append(res16("DW"))
            dw_x[0] = ct["DW"]
            g.dma_start(out=wslot[0][:, :, :],
                        in_=w_h["wq1t"].ap().rearrange("(ic p) o -> p ic o", p=128)
                        ).then_inc(sDW, 16)
            dw_wq1[0] = res16("DW")
            g.dma_start(out=b2sb, in_=b2_h.ap()).then_inc(sDW, 16)
            res16("DW")
            g.dma_start(out=rowb3, in_=rowb3_h.ap()).then_inc(sDW, 16)
            dw_pre[0] = res16("DW")
            g.iota(icol, pattern=[[1, 128]], base=0, channel_multiplier=0,
                   allow_small_or_imprecise_dtypes=True)
            g.iota(irow, pattern=[[0, 128]], base=0, channel_multiplier=1,
                   allow_small_or_imprecise_dtypes=True)
            g.iota(tie, pattern=[[1, 32]], base=0, channel_multiplier=0,
                   allow_small_or_imprecise_dtypes=True).then_inc(sP, 1)

        do_smalls = ct["DO"] + 48

        @blk.sync
        def _(sy):
            sy.dma_start(out=bias4, in_=bias4_h.ap()).then_inc(sDO, 16)
            sy.dma_start(out=bcv, in_=bc_h.ap()).then_inc(sDO, 16)
            sy.dma_start(out=b1p, in_=b1p_h.ap()).then_inc(sDO, 16)
            res16("DO"); res16("DO"); res16("DO")

        v_const = res("V")

        @blk.vector
        def _(v):
            v.wait_ge(sP, p_const)
            v.tensor_tensor(I128, icol, irow, op=ALU.is_equal)
            v.memset(ones128, 1.0)
            v.memset(ones1, 1.0)
            v.memset(zcol, 0.0)
            v.memset(onef, 1.0)
            v.tensor_scalar_mul(tie, tie, 1e-6 / 31.0).then_inc(sV, 1)

        def layer_norm(src, dst, waits, per_chunk=None):
            s_stats = res("S")

            @blk.scalar
            def _(s):
                for sem, val in waits:
                    s.wait_ge(sem, val)
                ins = None
                for c in range(NTC):
                    if per_chunk is not None:
                        s.wait_ge(per_chunk[0], per_chunk[1][c])
                    ins = s.activation(scr0, src[:, c, :], AF.Square,
                                       accum_out=ssq[:, c:c + 1])
                ins.then_inc(sS, 1)

            vv = res("V", 5)
            v_var = vv[4]

            @blk.vector
            def _(v):
                ins = None
                for c in range(NTC):
                    if per_chunk is not None:
                        v.wait_ge(per_chunk[0], per_chunk[1][c])
                    ins = v.reduce_sum(ssum[:, c:c + 1], src[:, c, :], axis=AX.X)
                ins.then_inc(sV, 1)
                v.wait_ge(sS, s_stats)
                v.wait_ge(sV, vv[0])
                v.tensor_scalar_mul(smean, ssum, 1.0 / D).then_inc(sV, 1)
                v.wait_ge(sV, vv[1])
                v.tensor_tensor(svar, smean, smean, op=ALU.mult).then_inc(sV, 1)
                v.wait_ge(sV, vv[2])
                v.scalar_tensor_tensor(svar, ssq, 1.0 / D, svar,
                                       op0=ALU.mult, op1=ALU.subtract).then_inc(sV, 1)
                v.wait_ge(sV, vv[3])
                v.tensor_scalar_add(svar, svar, EPS).then_inc(sV, 1)

            s_ln = res("S")

            @blk.scalar
            def _(s):
                s.wait_ge(sV, v_var)
                s.activation(svar, svar, AF.Ln).then_inc(sS, 1)

            v_half = res("V")

            @blk.vector
            def _(v):
                v.wait_ge(sS, s_ln)
                v.tensor_scalar_mul(svar, svar, -0.5).then_inc(sV, 1)

            s_rstd = res("S")

            @blk.scalar
            def _(s):
                s.wait_ge(sV, v_half)
                s.activation(srstd, svar, AF.Exp).then_inc(sS, 1)

            vn = res("V", 5)

            @blk.vector
            def _(v):
                v.wait_ge(sS, s_rstd)
                v.tensor_scalar_mul(smean, smean, -1.0).then_inc(sV, 1)
                v.wait_ge(sV, vn[0])
                for grp in range(4):
                    ins = None
                    for c in range(4 * grp, 4 * grp + 4):
                        ins = v.tensor_scalar(dst[:, c, :], src[:, c, :],
                                              smean[:, c:c + 1], srstd[:, c:c + 1],
                                              op0=ALU.add, op1=ALU.mult)
                    ins.then_inc(sV, 1)
            return vn[1:]

        def transpose_to_feat(src_tok, dst_feat, ready_v):
            pT = [pB[:, 1024:1536].bitcast(BF16), pB[:, 1536:2048].bitcast(BF16)]
            n = NTC * NDC
            tr_ms = res("T", n)
            ev_ms = res("V", n)
            rv = ready_v if isinstance(ready_v, list) else [ready_v] * 4

            @blk.tensor
            def _(t):
                last = -1
                for idx in range(n):
                    c, k = divmod(idx, NDC)
                    if c // 4 != last:
                        last = c // 4
                        t.wait_ge(sV, rv[last])
                    if idx >= 2:
                        t.wait_ge(sV, ev_ms[idx - 2])
                    t.transpose(pT[idx % 2][:, 0:128],
                                src_tok[:, c, 128 * k:128 * (k + 1)],
                                I128).then_inc(sT, 1)

            @blk.vector
            def _(v):
                for idx in range(n):
                    c, k = divmod(idx, NDC)
                    v.wait_ge(sT, tr_ms[idx])
                    v.tensor_copy(dst_feat[:, k, 128 * c:128 * (c + 1)],
                                  pT[idx % 2][:, 0:128]).then_inc(sV, 1)
            return ev_ms[-1]

        def proj_feat(ws_ap, ws_dw, dst, bias_idx, extra_t_waits=()):
            mm_ms = res("T", NDC)
            ev_ms = res("S", NDC)

            @blk.tensor
            def _(t):
                t.wait_ge(sV, v_xnf[0])
                t.wait_ge(sDW, ws_dw)
                for sem, val in extra_t_waits:
                    t.wait_ge(sem, val)
                for oc in range(NDC):
                    pm = pA if (oc % 2 == 0) else pB
                    if oc >= 2:
                        t.wait_ge(sS, ev_ms[oc - 2])
                    ins = None
                    for ic in range(NDC):
                        for nb in range(4):
                            ins = t.matmul(pm[:, 512 * nb:512 * (nb + 1)],
                                           ws_ap[:, ic, 128 * oc:128 * (oc + 1)],
                                           xnf[:, ic, 512 * nb:512 * (nb + 1)],
                                           start=(ic == 0), stop=(ic == NDC - 1))
                    ins.then_inc(sT, 1)

            @blk.scalar
            def _(s):
                s.wait_ge(sDO, do_smalls)
                for oc in range(NDC):
                    pm = pA if (oc % 2 == 0) else pB
                    s.wait_ge(sT, mm_ms[oc])
                    ins = None
                    for nb in range(4):
                        ins = s.activation(dst[:, oc, 512 * nb:512 * (nb + 1)],
                                           pm[:, 512 * nb:512 * (nb + 1)], AF.Gelu,
                                           bias=bias4[:, 8 * bias_idx + oc:8 * bias_idx + oc + 1])
                    ins.then_inc(sS, 1)
            return mm_ms[-1], ev_ms[-1]

        def proj_tok_softplus(ws_ap, ws_dw, src_feat, src_ready_s, rb_idx, dst):
            mm_ms = res("T", NTC)
            ev3 = res("S", 3 * NTC)
            ev_ms = ev3[2::3]

            @blk.tensor
            def _(t):
                t.wait_ge(sS, src_ready_s)
                t.wait_ge(sDW, ws_dw)
                t.wait_ge(sDW, dw_pre[0])
                for tc in range(NTC):
                    pm = pA if (tc % 2 == 0) else pB
                    if tc >= 2:
                        t.wait_ge(sS, ev_ms[tc - 2])
                    for ic in range(NDC):
                        for nh in range(2):
                            t.matmul(pm[:, 512 * nh:512 * (nh + 1)],
                                     src_feat[:, ic, 128 * tc:128 * (tc + 1)],
                                     ws_ap[:, ic, 512 * nh:512 * (nh + 1)],
                                     start=(ic == 0), stop=False)
                    ins = None
                    for nh in range(2):
                        ins = t.matmul(pm[:, 512 * nh:512 * (nh + 1)], ones1,
                                       rowb3[0:1, D * rb_idx + 512 * nh:D * rb_idx + 512 * (nh + 1)],
                                       start=False, stop=True)
                    ins.then_inc(sT, 1)

            @blk.scalar
            def _(s):
                s.wait_ge(sV, v_const)
                for tc in range(NTC):
                    pm = pA if (tc % 2 == 0) else pB
                    s.wait_ge(sT, mm_ms[tc])
                    for nh in range(2):
                        s.activation(pm[:, 1024 + 512 * nh:1024 + 512 * (nh + 1)],
                                     pm[:, 512 * nh:512 * (nh + 1)],
                                     AF.Exp).then_inc(sS, 1)
                        s.wait_ge(sS, ev3[3 * tc + nh])
                        ins = s.activation(dst[:, tc, 512 * nh:512 * (nh + 1)],
                                           pm[:, 1024 + 512 * nh:1024 + 512 * (nh + 1)],
                                           AF.Ln, bias=onef)
                    ins.then_inc(sS, 1)
            return mm_ms[-1], ev_ms[-1]

        def prefetch(slot_i, name, gate_T):
            dw = None

            @blk.gpsimd
            def _(g):
                if gate_T:
                    g.wait_ge(sT, gate_T)
                g.dma_start(out=wslot[slot_i][:, :, :],
                            in_=w_h[name].ap().rearrange("(ic p) o -> p ic o", p=128)
                            ).then_inc(sDW, 16)
            return res16("DW")

        tapx = [0]

        @blk.gpsimd
        def _(g):
            pass
        v_ln1g = layer_norm(xt, xn_tok, [],
                            per_chunk=(sDW, [dw_xchunks[c // 4] for c in range(NTC)]))
        v_ln1 = v_ln1g[-1]
        tap("xt", sDW, dw_x[0], A[:, P0:P0 + 16384])
        tap("xn", sV, v_ln1, A[:, P1:P1 + 16384])
        tap("stats", sV, v_ln1, A[:, E_ss:E_ss + 160].bitcast(F32))
        v_xnf = [0]
        v_xnf[0] = transpose_to_feat(xn_tok, xnf, v_ln1g)

        tap("xnf", sV, v_xnf[0], A[:, P0:P0 + 16384])
        dw_wk1 = prefetch(1, "wk1t", 0)
        q1_mm, q1_ev = proj_feat(wslot[0], dw_wq1[0], q1h, 0)
        tap("q1h", sS, q1_ev, A[:, P2:P2 + 16384])
        dw_wq2 = prefetch(0, "wq2t", q1_mm)
        k1_mm, k1_ev = proj_feat(wslot[1], dw_wk1, k1h, 1)
        dw_wk2 = prefetch(1, "wk2t", k1_mm)
        q2_mm, q2_ev = proj_tok_softplus(wslot[0], dw_wq2, q1h, q1_ev, 0, q_tok)
        tap("qtok", sS, q2_ev, A[:, P1:P1 + 16384])
        dw_wv = prefetch(0, "wvt", q2_mm)
        k2_mm, k2_ev = proj_tok_softplus(wslot[1], dw_wk2, k1h, k1_ev, 1, k_tok)
        tap("ktok", sS, k2_ev, A[:, P4:P4 + 16384])
        dw_wg = prefetch(1, "wgt", k2_mm)

        @blk.gpsimd
        def _(g):
            for i in range(4):
                g.dma_start(out=w1b.ap()[:, 11 * i:11 * (i + 1) if i < 3 else NR],
                            in_=w1p_h.ap()[:, 11 * i:11 * (i + 1) if i < 3 else NR]
                            ).then_inc(sDW, 16)
                res16("DW")
                g.dma_start(out=w2b.ap()[:, 11 * i:11 * (i + 1) if i < 3 else NR],
                            in_=w2p_h.ap()[:, 11 * i:11 * (i + 1) if i < 3 else NR]
                            ).then_inc(sDW, 16)
                res16("DW")
        dw_wcast = ct["DW"]


        a_mm = res("T")

        @blk.tensor
        def _(t):
            t.wait_ge(sS, q2_ev)
            t.wait_ge(sS, k2_ev)
            ins = None
            for p in range(8):
                for tc in range(NTC):
                    ins = t.matmul(pA[:, 128 * p:128 * (p + 1)],
                                   k_tok[:, tc, 128 * p:128 * (p + 1)],
                                   q_tok[:, tc, 128 * p:128 * (p + 1)],
                                   start=(tc == 0), stop=(tc == NTC - 1))
            ins.then_inc(sT, 1)

        v_aloc = res("V")

        @blk.vector
        def _(v):
            v.wait_ge(sT, a_mm)
            v.tensor_copy(aloc, pA[:, 0:1024]).then_inc(sV, 1)

        do_adram = res16("DO")

        @blk.sync
        def _(sy):
            sy.wait_ge(sV, v_aloc)
            sy.dma_start(out=a_dram.ap(), in_=aloc).then_inc(sDO, 16)

        tap("aloc", sV, v_aloc, aloc)
        xrr = xres_h.ap().rearrange("(c p) d -> p c d", p=128)

        @blk.gpsimd
        def _(g):
            g.wait_ge(sDO, do_adram)
            g.collective_compute(
                "AllReduce", ALU.add, replica_groups=[list(range(NCORE))],
                ins=[a_dram.ap()], outs=[a_red.ap()]).then_inc(sCC, 1)

        @blk.gpsimd
        def _(g):
            g.wait_ge(sT, a_mm)
            for i in range(4):
                g.dma_start(out=x2[:, 4 * i:4 * (i + 1), :],
                            in_=xrr[:, 4 * i:4 * (i + 1), :]).then_inc(sDW, 16)
                res16("DW")
        dw_xrl = ct["DW"]

        @blk.sync
        def _(sy):
            sy.wait_ge(sCC, 1)
            sy.dma_start(out=aloc, in_=a_red.ap()).then_inc(sAR, 16)

        tap("ared", sAR, 16, aloc)
        v_mm, v_ev = proj_feat(wslot[0], dw_wv, vfe, 2,
                               extra_t_waits=[(sV, v_aloc)])
        dw_wp = prefetch(0, "wpt", v_mm)
        g_mm, g_ev = proj_feat(wslot[1], dw_wg, gfe, 3)

        vbd2 = res("V", 2)
        v_bd = vbd2[1]

        @blk.vector
        def _(v):
            v.memset(A[:, E_BD:E_BD + 1024], 0.0).then_inc(sV, 1)
            v.wait_ge(sV, vbd2[0])
            v.wait_ge(sAR, 16)
            ar = aloc.rearrange("p (a b) -> p a b", b=128)
            ins = None
            for p in range(8):
                v.tensor_scalar_mul(BD[0:64, p, 0:64], ar[0:64, p, 0:64], SCALE)
                ins = v.tensor_scalar_mul(BD[64:128, p, 64:128],
                                          ar[64:128, p, 64:128], SCALE)
            ins.then_inc(sV, 1)

        tap("bd", sV, v_bd, A[:, E_BD:E_BD + 1024])
        tap("vfe", sS, v_ev, A[:, P2:P2 + 16384])
        tap("gfe", sS, g_ev, A[:, P3:P3 + 16384])
        z_mm = res("T")

        @blk.tensor
        def _(t):
            t.wait_ge(sV, v_bd)
            t.wait_ge(sS, g_ev)
            ins = None
            for p in range(8):
                ins = t.matmul(pB[:, p:p + 1], BD[:, p, :], ones128,
                               start=True, stop=True)
            ins.then_inc(sT, 1)

        vz2 = res("V", 2)
        v_z = vz2[1]

        @blk.vector
        def _(v):
            v.wait_ge(sT, z_mm)
            v.tensor_scalar_add(zsb, pB[:, 0:8], 1024.0).then_inc(sV, 1)
            v.wait_ge(sV, vz2[0])
            v.reciprocal(zsb, zsb).then_inc(sV, 1)

        tap("z", sV, v_z, zsb)
        o_mm = res("T", NDC)
        o_ev = res("V", NDC)

        @blk.tensor
        def _(t):
            t.wait_ge(sV, v_z)
            for p in range(NDC):
                pm = pA if (p % 2 == 0) else pB
                if p >= 2:
                    t.wait_ge(sV, o_ev[p - 2])
                ins = None
                for nb in range(4):
                    t.matmul(pm[:, 512 * nb:512 * (nb + 1)], BD[:, p, :],
                             vfe[:, p, 512 * nb:512 * (nb + 1)],
                             start=True, stop=False)
                    ins = t.matmul(pm[:, 512 * nb:512 * (nb + 1)], I128,
                                   vfe[:, p, 512 * nb:512 * (nb + 1)],
                                   start=False, stop=True)
                ins.then_inc(sT, 1)

        @blk.vector
        def _(v):
            for p in range(NDC):
                pm = pA if (p % 2 == 0) else pB
                v.wait_ge(sT, o_mm[p])
                opv = Opp[:, p, :].rearrange("p (m a) -> p a m", m=16)
                ins = None
                for nb in range(4):
                    ins = v.scalar_tensor_tensor(
                        opv[:, 32 * nb:32 * (nb + 1), :],
                        pm[:, 512 * nb:512 * (nb + 1)]
                        .rearrange("p (a m) -> p a m", m=16),
                        zsb[:, p:p + 1],
                        gfe[:, p, 512 * nb:512 * (nb + 1)]
                        .rearrange("p (a m) -> p a m", m=16),
                        op0=ALU.mult, op1=ALU.mult)
                ins.then_inc(sV, 1)

        tap("opp", sV, o_ev[-1], A[:, P0:P0 + 16384])
        o2s_wr = [0] * NDC

        @blk.gpsimd
        def _(sy):
            for p in range(NDC):
                sy.wait_ge(sV, o_ev[p])
                for h2 in range(2):
                    o2s_out = bass.AP(o2s.ap().tensor, (2 * p + h2) * 131072,
                                      [[128, 64], [16384, 8], [8192, 2], [1, 128]])
                    sy.dma_start(
                        out=o2s_out,
                        in_=Opp[64 * h2:64 * (h2 + 1), p, :]
                        .rearrange("p (c t a) -> p c t a", c=8, t=2)
                    ).then_inc(sO2S, 16)
                o2s_wr[p] = 16 * 2 * (p + 1)

        sfeat = v3(P0, NDC, TOK)
        do_sf = [0]

        @blk.sync
        def _(sy):
            sy.wait_ge(sO2S, o2s_wr[-1])
            for ci in range(8):
                sf_in = bass.AP(
                    o2s.ap().tensor, 16384 * ci,
                    [[128, 128], [131072, 16], [1, 128]],
                )
                sy.dma_start(out=sfeat[:, ci, :], in_=sf_in).then_inc(sSF, 16)
            do_sf[0] = 128

        p_mm = res("T", NTC)
        p_ev = res("V", NTC)

        @blk.tensor
        def _(t):
            t.wait_ge(sSF, do_sf[0])
            t.wait_ge(sDW, dw_wp)
            t.wait_ge(sDW, dw_xrl)
            for tc in range(NTC):
                pm = pA if (tc % 2 == 0) else pB
                if tc >= 2:
                    t.wait_ge(sV, p_ev[tc - 2])
                for ic in range(NDC):
                    for nh in range(2):
                        t.matmul(pm[:, 512 * nh:512 * (nh + 1)],
                                 sfeat[:, ic, 128 * tc:128 * (tc + 1)],
                                 wslot[0][:, ic, 512 * nh:512 * (nh + 1)],
                                 start=(ic == 0), stop=False)
                ins = None
                for nh in range(2):
                    ins = t.matmul(pm[:, 512 * nh:512 * (nh + 1)], ones1,
                                   rowb3[0:1, 2 * D + 512 * nh:2 * D + 512 * (nh + 1)],
                                   start=False, stop=True)
                ins.then_inc(sT, 1)

        @blk.vector
        def _(v):
            for tc in range(NTC):
                pm = pA if (tc % 2 == 0) else pB
                v.wait_ge(sT, p_mm[tc])
                v.scalar_tensor_tensor(x2[:, tc, :], pm[:, 0:1024], 0.0,
                                       x2[:, tc, :], op0=ALU.bypass,
                                       op1=ALU.add).then_inc(sV, 1)
        v_x2 = p_ev[-1]

        if STAGE == "attn":
            @blk.gpsimd
            def _(g):
                g.wait_ge(sV, v_x2)
                for c in range(NTC):
                    g.dma_start(out=xo[:, c, :], in_=x2[:, c, :]).then_inc(sDW, 16)
                    res16("DW")
                g.wait_ge(sDW, ct["DW"])
            return nc

        def _dump_x2_and_end(wait_list):
            @blk.gpsimd
            def _(g):
                for sem, val in wait_list:
                    g.wait_ge(sem, val)
                for c in range(NTC):
                    g.dma_start(out=xo[:, c, :], in_=x2[:, c, :]).then_inc(sDW, 16)
                    res16("DW")
                g.wait_ge(sDW, ct["DW"])

        v_ln2g = layer_norm(x2, yn_tok, [], per_chunk=(sV, p_ev))
        v_ln2 = v_ln2g[-1]
        v_ynf = transpose_to_feat(yn_tok, ynf, v_ln2g)
        if LEVEL <= 1:
            _dump_x2_and_end([(sV, v_ynf)])
            return nc

        @blk.gpsimd
        def _(g):
            g.wait_ge(sT, p_mm[-1])
            g.dma_start(out=wcsb[:, :, :],
                        in_=wc_h.ap().rearrange("(ic p) e -> p ic e", p=128)
                        ).then_inc(sDW, 16)
        dw_wc = res16("DW")

        rt_mm = res("T")

        @blk.tensor
        def _(t):
            t.wait_ge(sV, v_ynf)
            t.wait_ge(sDW, dw_wc)
            ins = None
            for ic in range(NDC):
                for nb in range(4):
                    ins = t.matmul(pA[:, 512 * nb:512 * (nb + 1)], wcsb[:, ic, :],
                                   ynf[:, ic, 512 * nb:512 * (nb + 1)],
                                   start=(ic == 0), stop=(ic == NDC - 1))
            ins.then_inc(sT, 1)

        vp3 = res("V", 3)
        v_perm = vp3[2]

        @blk.vector
        def _(v):
            v.wait_ge(sT, rt_mm)
            v.wait_ge(sDO, do_smalls)
            lg = pA[:, 0:2048].rearrange("p (g w) -> p g w", w=32)
            v.scalar_tensor_tensor(lg, lg, bcv[:, 0:1],
                                   tie.unsqueeze(1).broadcast_to([128, NG, 32]),
                                   op0=ALU.add, op1=ALU.add).then_inc(sV, 1)
            v.wait_ge(sV, vp3[0])
            v.reduce_max(gmax.unsqueeze(2), lg, axis=AX.X).then_inc(sV, 1)
            v.wait_ge(sV, vp3[1])
            v.tensor_tensor(perm.rearrange("p (g w) -> p g w", w=32), lg,
                            gmax.unsqueeze(2).broadcast_to([128, NG, 32]),
                            op=ALU.is_equal).then_inc(sV, 1)

        if LEVEL <= 2:
            _dump_x2_and_end([(sV, v_perm)])
            return nc
        tap("yn", sV, v_ln2, A[:, P4:P4 + 16384])
        tap("perm", sV, v_perm, A[:, P0 + F_PERM:P0 + F_PERM + 2048])
        tap("gmax", sV, v_perm, A[:, E_gmax:E_gmax + 128].bitcast(F32))
        pt_ms = res("T", NTC)
        ptv_ms = res("V", NTC)
        pTa = pB[:, 1024:1536].bitcast(BF16)
        pTb = pB[:, 1536:2048].bitcast(BF16)

        @blk.tensor
        def _(t):
            t.wait_ge(sV, v_perm)
            for c in range(NTC):
                if c >= 2:
                    t.wait_ge(sV, ptv_ms[c - 2])
                t.transpose((pTa if c % 2 == 0 else pTb)[:, 0:128],
                            perm[:, 128 * c:128 * (c + 1)], I128).then_inc(sT, 1)

        @blk.vector
        def _(v):
            for c in range(NTC):
                v.wait_ge(sT, pt_ms[c])
                v.tensor_copy(permT[:, c, :],
                              (pTa if c % 2 == 0 else pTb)[:, 0:128]).then_inc(sV, 1)
        v_permt = ptv_ms[-1]

        if LEVEL <= 3:
            _dump_x2_and_end([(sV, v_permt)])
            return nc
        NHALF = 2 if LEVEL >= 10 else 1
        wblk_dw_all = []
        blk_gate = []

        half_res = []
        for h in range(NHALF):
            hr = {}
            hr["v_pbig2"] = res("V", 2)
            hr["v_pbig"] = hr["v_pbig2"][1]
            hr["gx_ms"] = res("T", 64)
            hr["gx_ev"] = res("V", 64)
            hr["w1_ms"] = []
            hr["w2_ms"] = []
            for r in range(NR):
                hr["w1_ms"].append(res("T"))
                hr["w2_ms"].append(res("T"))
            hr["in_ev"] = [res("S") for r in range(NR)]
            hr["r3_ms"] = [res("V") for r in range(NR)]
            hr["sc_ms"] = res("T", 8)
            hr["sc_ev"] = res("V", 8)
            half_res.append(hr)
        glb = []
        for h in range(NHALF):
            for (rb0, rbn) in BLOCKS:
                glb.append(half_res[h]["w2_ms"][rb0 + rbn - 1])

        wdma_ms = []

        @blk.gpsimd
        def _(g):
            if LEVEL <= 4:
                return
            gi = 0
            for h in range(NHALF):
                for bi, (rb0, rbn) in enumerate(BLOCKS):
                    sl = bi % 2
                    if gi >= 2 and LEVEL >= 6:
                        g.wait_ge(sT, glb[gi - 2])
                    else:
                        g.wait_ge(sT, rt_mm)
                    ws = sWS[gi % 2]
                    wsk = f"WS{gi % 2}"
                    if gi == 0:
                        g.wait_ge(sDW, dw_wcast)
                    g.dma_start(out=A[:, P5 + F_W1[sl]:P5 + F_W1[sl] + rbn * 768],
                                in_=w1b.ap()[:, rb0:rb0 + rbn, :, :, :]
                                .rearrange("p r q k f -> p (r q k f)")
                                ).then_inc(ws, 16)
                    xsem[wsk] += 16
                    g.dma_start(out=A[:, P5 + F_W2[sl]:P5 + F_W2[sl] + rbn * 1024],
                                in_=w2b.ap()[:, rb0:rb0 + rbn, :]
                                .rearrange("p r d -> p (r d)")).then_inc(ws, 16)
                    xsem[wsk] += 16
                    wdma_ms.append((ws, xsem[wsk]))
                    gi += 1

        for h in range(NHALF):
            hr = half_res[h]
            hr["r3_dma"] = [None] * NR
            hr["r2_dma"] = [None] * 8
            hr["out_dma"] = [None] * 8

        for h in range(NHALF):
            hr = half_res[h]
            prev = half_res[h - 1] if h == 1 else None

            @blk.vector
            def _(v, h=h, hr=hr, prev=prev):
                v.wait_ge(sV, v_permt)
                if prev is not None:
                    v.wait_ge(sT, prev["sc_ms"][-1])
                    v.wait_ge(*prev["out_dma"][-1])
                if h == 0:
                    v.memset(A[:, P5 + 11264:P5 + 11264 + 192], 0.0)
                v.memset(A[:, P0 + F_PBIG:P0 + F_PBIG + 4096], 0.0).then_inc(sV, 1)
                v.wait_ge(sV, hr["v_pbig2"][0])
                ins = None
                for c8 in range(8):
                    c = 8 * h + c8
                    for j in range(4):
                        ins = v.tensor_copy(
                            pbig[32 * j:32 * (j + 1), c8, 128 * j:128 * (j + 1)],
                            permT[32 * j:32 * (j + 1), c, :])
                ins.then_inc(sV, 1)

            @blk.tensor
            def _(t, h=h, hr=hr):
                t.wait_ge(sV, hr["v_pbig"])
                for idx in range(64):
                    c8, k = divmod(idx, 8)
                    c = 8 * h + c8
                    sl = idx % 2
                    if idx >= 2:
                        t.wait_ge(sV, hr["gx_ev"][idx - 2])
                    t.matmul(pB[:, 512 * sl:512 * (sl + 1)],
                             yn_tok[:, c, 128 * k:128 * (k + 1)],
                             pbig[:, c8, :], start=True, stop=True).then_inc(sT, 1)

            @blk.vector
            def _(v, h=h, hr=hr):
                for idx in range(64):
                    c8, k = divmod(idx, 8)
                    sl = idx % 2
                    v.wait_ge(sT, hr["gx_ms"][idx])
                    base = (P2 if k < 4 else P3) + (k % 4) * 4096 + 512 * c8
                    v.tensor_copy(A[:, base:base + 512],
                                  pB[:, 512 * sl:512 * (sl + 1)]).then_inc(sV, 1)

            if h == 0:
                tap("xlo", sV, hr["gx_ev"][-1], A[:, P2:P2 + 16384], eng="sync")
                tap("xhi", sV, hr["gx_ev"][-1], A[:, P3:P3 + 16384], eng="sync")
            if LEVEL <= 4:
                _dump_x2_and_end([(sV, hr["gx_ev"][-1])])
                return nc
            if LEVEL == 5:
                _dump_x2_and_end([(sV, hr["gx_ev"][-1]), (sDW, wdma_ms[-1])])
                return nc
            PINS = [(pA, 0), (pA, 512), (pB, 0)]

            @blk.tensor
            def _(t, h=h, hr=hr, PINS=PINS):
                t.wait_ge(sV, hr["gx_ev"][-1])
                for bi, (rb0, rbn) in enumerate(BLOCKS):
                    t.wait_ge(*wdma_ms[h * NB + bi])
                    sl = bi % 2
                    for rr in range(rbn):
                        r = rb0 + rr
                        nq = 3 if r < NR - 1 else 2
                        rdb = r % 2
                        pw = (pA if rdb == 0 else pB)[:, 1024:2048]
                        r4 = r % 4
                        if r >= 1:
                            t.wait_ge(sS, hr["in_ev"][r - 1])
                        ins = None
                        for k in range(8):
                            for q in range(nq):
                                es = 3 * r + q
                                pin, pco = PINS[q]
                                ins = t.matmul(pin[32 * q:32 * (q + 1), pco:pco + 32],
                                               w1blk[sl][:, rr, q, k, :],
                                               x_ap(h, es, k),
                                               start=(k == 0), stop=(k == 7),
                                               tile_position=(0, 32 * q))
                        ins.then_inc(sT, 1)
                        t.wait_ge(sS, hr["in_ev"][r])
                        if LEVEL >= 7:
                            if r >= 2:
                                t.wait_ge(sV, hr["r3_ms"][r - 2])
                            ins = None
                            for nh in range(2):
                                ins = t.matmul(pw[0:96, 512 * nh:512 * (nh + 1)],
                                               bdin[rdb][0:96, :],
                                               w2blk[sl][0:96, rr,
                                                         512 * nh:512 * (nh + 1)],
                                               start=True, stop=True)
                            ins.then_inc(sT, 1)
                        else:
                            t.nop().then_inc(sT, 1)

            @blk.scalar
            def _(s_, h=h, hr=hr, PINS=PINS):
                for r in range(NR):
                    nq = 3 if r < NR - 1 else 2
                    rdb = r % 2
                    r4 = r % 4
                    pw = (pA if rdb == 0 else pB)[:, 1024:2048]
                    s_.wait_ge(sT, hr["w1_ms"][r])
                    ins = None
                    for q in range(nq):
                        pin, pco = PINS[q]
                        ins = s_.activation(bdin[rdb][32 * q:32 * (q + 1),
                                                      32 * q:32 * (q + 1)],
                                            pin[32 * q:32 * (q + 1), pco:pco + 32],
                                            AF.Relu,
                                            bias=b1p[32 * q:32 * (q + 1), r:r + 1])
                    ins.then_inc(sS, 1)


            @blk.vector
            def _(v, h=h, hr=hr):
                for r in range(NR):
                    rdb = r % 2
                    r4 = r % 4
                    pw = (pA if rdb == 0 else pB)[:, 1024:2048]
                    if LEVEL >= 7:
                        v.wait_ge(sT, hr["w2_ms"][r])
                        if LEVEL >= 8 and r >= 4:
                            gr2 = h * NR + r - 4
                            v.wait_ge(sR3[gr2 % 3], 16 * (gr2 // 3 + 1))
                        v.tensor_copy(r3[r4][0:96, :], pw[0:96, 0:1024]).then_inc(sV, 1)
                    else:
                        v.nop().then_inc(sV, 1)

            @blk.sync
            def _(sy, h=h, hr=hr):
                if LEVEL < 8:
                    return
                for r in range(NR):
                    nq = 3 if r < NR - 1 else 2
                    r4 = r % 4
                    gr = h * NR + r
                    sy.wait_ge(sV, hr["r3_ms"][r])
                    sem = sR3[gr % 3]
                    xsem[f"R3{gr % 3}"] += 16
                    hr["r3_dma"][r] = (sem, xsem[f"R3{gr % 3}"])
                    sy.dma_start(
                        out=r2d.ap()[3 * r:3 * r + nq, 32 * h:32 * (h + 1), :],
                        in_=r3[r4][0:32 * nq, :].rearrange(
                            "p (q d) -> p q d", q=1)).then_inc(sem, 16)

            if h == 0:
                if TAP == "r2d":
                    @blk.sync
                    def _(sy, hr=hr):
                        sy.wait_ge(sDO, hr["r3_dma"][-1])
                        sy.dma_start(out=tap2_h.ap()[:, 0:8192],
                                     in_=r2d.ap()[:, 0:8, :].rearrange(
                                         "e g d -> e (g d)")).then_inc(sDO, 16)
                        res16("DO")
            if LEVEL in (6, 7):
                _dump_x2_and_end([(sS, hr["in_ev"][-1]), (sV, hr["r3_ms"][-1])])
                return nc
            if LEVEL == 8:
                _dump_x2_and_end([(sDO, hr["r3_dma"][-1])])
                return nc
            r3fin = [(sR3[i], xsem[f"R3{i}"]) for i in range(3)]

            @blk.scalar
            def _(sy, h=h, hr=hr, r3fin=r3fin):
                for c8 in range(8):
                    c = 8 * h + c8
                    sl = c8 % 4
                    gc = 8 * h + c8
                    if c8 == 0:
                        for sem, val in r3fin:
                            sy.wait_ge(sem, val)
                    if c8 >= 4:
                        sy.wait_ge(sT, hr["sc_ms"][c8 - 4])
                    sem = sR2[gc % 2]
                    xsem[f"R2{gc % 2}"] += 16
                    hr["r2_dma"][c8] = (sem, xsem[f"R2{gc % 2}"])
                    sy.dma_start(out=r2blk[sl][:, :, :],
                                 in_=r2d.ap()[:, 4 * c:4 * (c + 1), :]
                                 ).then_inc(sem, 16)

            @blk.sync
            def _(sy, h=h, hr=hr):
                for c8 in range(8):
                    cc = 8 * h + c8
                    sy.wait_ge(sV, hr["sc_ev"][c8])
                    sem = sOD[cc % 2]
                    xsem[f"OD{cc % 2}"] += 16
                    hr["out_dma"][c8] = (sem, xsem[f"OD{cc % 2}"])
                    sy.dma_start(out=xo[:, cc, :],
                                 in_=outst[c8 % 4]).then_inc(sem, 16)

            @blk.tensor
            def _(t, h=h, hr=hr):
                for c8 in range(8):
                    sl = c8 % 4
                    pm = pA if c8 % 2 == 0 else pB
                    t.wait_ge(*hr["r2_dma"][c8])
                    if c8 >= 2:
                        t.wait_ge(sV, hr["sc_ev"][c8 - 2])
                    c = 8 * h + c8
                    ins = None
                    for j in range(4):
                        g = 4 * c + j
                        for nh in range(2):
                            t.matmul(pm[32 * j:32 * (j + 1), 512 * nh:512 * (nh + 1)],
                                     perm[:, 32 * g:32 * (g + 1)],
                                     b2sb[:, 512 * nh:512 * (nh + 1)],
                                     start=True, stop=False,
                                     tile_position=(0, 32 * j))
                            ins = t.matmul(pm[32 * j:32 * (j + 1), 512 * nh:512 * (nh + 1)],
                                           perm[:, 32 * g:32 * (g + 1)],
                                           r2blk[sl][:, j, 512 * nh:512 * (nh + 1)],
                                           start=False, stop=True,
                                           tile_position=(0, 32 * j))
                    ins.then_inc(sT, 1)

            @blk.vector
            def _(v, h=h, hr=hr):
                for c8 in range(8):
                    c = 8 * h + c8
                    sl = c8 % 2
                    pm = pA if c8 % 2 == 0 else pB
                    v.wait_ge(sT, hr["sc_ms"][c8])
                    if c8 >= 4:
                        v.wait_ge(*hr["out_dma"][c8 - 4])
                    v.scalar_tensor_tensor(outst[c8 % 4], pm[:, 0:1024], 0.0,
                                           x2[:, c, :], op0=ALU.bypass,
                                           op1=ALU.add).then_inc(sV, 1)



        @blk.sync
        def _(sy):
            sy.wait_ge(sDO, ct["DO"])
            for i in range(2):
                sy.wait_ge(sOD[i], xsem[f"OD{i}"])
                sy.wait_ge(sR2[i], xsem[f"R2{i}"])
                sy.wait_ge(sWS[i], xsem[f"WS{i}"])
            for i in range(3):
                sy.wait_ge(sR3[i], xsem[f"R3{i}"])
    return nc


def _prep(inputs):
    g = {k: np.asarray(v, dtype=np.float32) for k, v in inputs.items()}
    x = g["L0_attn_x"].reshape(16384, 1024)
    wT = lambda w: np.ascontiguousarray(w.T)
    com = {
        "wq1t": wT(g["L0_attn_q_w1"]), "wk1t": wT(g["L0_attn_k_w1"]),
        "wq2t": wT(g["L0_attn_q_w2"]), "wk2t": wT(g["L0_attn_k_w2"]),
        "wvt": wT(g["L0_attn_v_w"]), "wgt": wT(g["L0_attn_g_w"]),
        "wpt": wT(g["L0_attn_p_w"]),
    }
    bias4 = np.stack([g["L0_attn_q_b1"].reshape(8, 128).T,
                      g["L0_attn_k_b1"].reshape(8, 128).T,
                      g["L0_attn_v_b"].reshape(8, 128).T,
                      g["L0_attn_g_b"].reshape(8, 128).T], axis=1)
    com["bias4"] = np.ascontiguousarray(bias4.reshape(128, 32))
    com["rowb3"] = np.ascontiguousarray(
        np.concatenate([g["L0_attn_q_b2"], g["L0_attn_k_b2"],
                        g["L0_attn_p_b"]]).reshape(1, 3072))
    com["wc"] = np.ascontiguousarray(g["L0_ffn_Wc"].reshape(1024, 128))
    com["bcv"] = np.ascontiguousarray(g["L0_ffn_bc"].reshape(128, 1))
    b1es = g["L0_ffn_b1"].reshape(128, 32)
    b1p = np.zeros((128, NR), np.float32)
    w1es = g["L0_ffn_W1"].reshape(128, 1024, 32)
    w1p = np.zeros((128, NR, 3, 8, 32), np.float32)
    w2es = g["L0_ffn_W2"].reshape(128, 1024, 32)
    w2p = np.zeros((128, NR, 1024), np.float32)
    for q in range(3):
        idx = np.arange(q, 128, 3)
        b1p[32 * q:32 * q + 32, :len(idx)] = b1es[idx].T
        w1p[:, :len(idx), q] = w1es[idx].reshape(len(idx), 8, 128, 32).transpose(2, 0, 1, 3)
        w2p[32 * q:32 * q + 32, :len(idx), :] = w2es[idx].transpose(2, 0, 1)
    com["b1p"] = b1p
    com["w1p"] = np.ascontiguousarray(w1p)
    com["w2p"] = np.ascontiguousarray(w2p)
    com["b2"] = np.ascontiguousarray(g["L0_ffn_b2"].reshape(128, 1024))
    in_maps = []
    for c in range(NCORE):
        m = dict(com)
        m["x"] = np.ascontiguousarray(x[c * TOK:(c + 1) * TOK])
        idx = (np.arange(16)[:, None] * 1024 + 128 * c
               + np.arange(128)[None, :]).ravel()
        m["xres"] = np.ascontiguousarray(x[idx])
        in_maps.append(m)
    return in_maps


TAP_OUT = None
TAP2_OUT = None


def kernel(**inputs):
    in_maps = _prep(inputs)
    nc = build_nc()
    res = run_bass_kernel_spmd(nc, in_maps, core_ids=list(range(NCORE)))
    if TAP:
        globals()["TAP_OUT"] = [np.asarray(r["tap"]) for r in res.results]
        globals()["TAP2_OUT"] = [np.asarray(r["tap2"]) for r in res.results]
    outp = np.empty((16384, 1024), np.float32)
    for c in range(NCORE):
        idx = (np.arange(16)[:, None] * 1024 + 128 * c
               + np.arange(128)[None, :]).ravel()
        outp[idx] = np.asarray(res.results[c]["out"])
    return outp.reshape(1, 16384, 1024)
